# version 15
# baseline (speedup 1.0000x reference)
from contextlib import ExitStack

import numpy as np
import ml_dtypes
import concourse.bass as bass
import concourse.mybir as mybir
from concourse.bass_utils import run_bass_kernel_spmd

F32 = mybir.dt.float32
BF16 = mybir.dt.bfloat16
AF = mybir.ActivationFunctionType
ALU = mybir.AluOpType
AXX = mybir.AxisListType.X

L = 4096
DM = 2048
INTER = 4096
H = 64
PD = 64
NST = 128
CH = 256
NCH = 16
CONV = 4352
OUTIN = 8512
DE = 8192
KC = 4

DEBUG = False
PHASES = "all"


def _bfarr(x):
    return np.ascontiguousarray(np.asarray(x).astype(ml_dtypes.bfloat16))


def host_constants():
    ident = np.eye(128, dtype=np.float32)
    ut = np.triu(np.ones((CH, CH), dtype=np.float32))
    ut_t = np.ascontiguousarray(ut.reshape(2, 128, CH).transpose(1, 0, 2))
    return {"c_ident": _bfarr(ident), "c_ut": ut_t}


class Sem:
    def __init__(self, h):
        self.h = h
        self.n = 0

    def inc(self, instr, k=1):
        instr.then_inc(self.h, k)
        self.n += k
        return instr

    def dma(self, instr):
        return self.inc(instr, 16)


class NS:
    pass


def build(debug=False):
    nc = bass.Bass("TRN2", target_bir_lowering=False)
    io = NS()
    io.x = nc.dram_tensor("x", [L, DM], F32, kind="ExternalInput").ap()
    io.W_in = nc.dram_tensor("W_in", [DM, OUTIN], F32, kind="ExternalInput").ap()
    io.conv_wT = nc.dram_tensor("conv_wT", [CONV, KC], F32, kind="ExternalInput").ap()
    io.conv_b = nc.dram_tensor("conv_b", [CONV], F32, kind="ExternalInput").ap()
    io.dt_bias = nc.dram_tensor("dt_bias", [H], F32, kind="ExternalInput").ap()
    io.A_log = nc.dram_tensor("A_log", [H], F32, kind="ExternalInput").ap()
    io.D = nc.dram_tensor("D", [H], F32, kind="ExternalInput").ap()
    io.norm_weight = nc.dram_tensor("norm_weight", [INTER], F32, kind="ExternalInput").ap()
    io.W_out = nc.dram_tensor("W_out", [INTER, DM], F32, kind="ExternalInput").ap()
    io.W_upT = nc.dram_tensor("W_upT", [DM, DE], F32, kind="ExternalInput").ap()
    io.W_gateT = nc.dram_tensor("W_gateT", [DM, DE], F32, kind="ExternalInput").ap()
    io.W_downT = nc.dram_tensor("W_downT", [DE, DM], F32, kind="ExternalInput").ap()
    io.c_ident = nc.dram_tensor("c_ident", [128, 128], BF16, kind="ExternalInput").ap()
    io.c_ut = nc.dram_tensor("c_ut", [128, 2, CH], F32, kind="ExternalInput").ap()
    io.out = nc.dram_tensor("out", [L, DM], F32, kind="ExternalOutput").ap()

    ARENA_WORDS = 57 * 1024 * 1024
    arenas = [nc.dram_tensor(f"scr_arena{i}", [ARENA_WORDS], F32, kind="Internal").ap()
              for i in range(2)]
    offs = [0, 0]

    def scratch(name, shape, dt, dbg=False, ar=0):
        if debug and dbg:
            return nc.dram_tensor(name, shape, dt, kind="ExternalOutput").ap()
        n = 1
        for s in shape:
            n *= s
        words = n if dt == F32 else (n + 1) // 2
        words = (words + 1023) // 1024 * 1024
        view = arenas[ar][offs[ar]:offs[ar] + words]
        offs[ar] += words
        assert offs[ar] <= ARENA_WORDS, name
        if dt == BF16:
            view = view.bitcast(BF16)
        assert len(shape) == 2
        return view[: shape[0] * shape[1]].rearrange("(a b) -> a b", b=shape[1])

    io.xn_tok = scratch("xn_tok", [L, DM], BF16, dbg=True)
    io.sz_tok = scratch("sz_tok", [L, INTER], F32, dbg=True)
    io.xbc_cm = scratch("xbc_cm", [CONV, L], F32, dbg=True)
    io.xc_cm = scratch("xc_cm", [INTER, L], BF16, dbg=True)
    io.yf_tok = scratch("yf_tok", [L, INTER], BF16, dbg=True)
    io.s_tok = scratch("s_tok", [L, DM], F32, dbg=True, ar=1)
    io.q_upT = scratch("q_upT", [DM, DE], BF16, dbg=True, ar=1)
    io.q_gateT = scratch("q_gateT", [DM, DE], BF16, ar=1)
    io.q_downT = scratch("q_downT", [DE, DM], BF16, ar=1)
    io.hid_cm = scratch("hid_cm", [DE, L], BF16, dbg=True, ar=1)
    io.srow = scratch("srow", [8, L], F32, ar=1)
    io.dbg = scratch("dbgout", [128, 16384], F32, dbg=True) if debug else None

    build_program(nc, io, debug)
    return nc


def kernel(**inputs):
    nc = build(DEBUG)
    consts = host_constants()
    shared = dict(
        W_in=np.ascontiguousarray(inputs["L0_attn_W_in"]),
        conv_wT=np.ascontiguousarray(np.asarray(inputs["L0_attn_conv_w"]).T),
        conv_b=np.ascontiguousarray(inputs["L0_attn_conv_b"]),
        dt_bias=np.ascontiguousarray(inputs["L0_attn_dt_bias"]),
        A_log=np.ascontiguousarray(inputs["L0_attn_A_log"]),
        D=np.ascontiguousarray(inputs["L0_attn_D"]),
        norm_weight=np.ascontiguousarray(inputs["L0_attn_norm_weight"]),
        W_out=np.ascontiguousarray(inputs["L0_attn_W_out"]),
        W_upT=np.ascontiguousarray(np.asarray(inputs["L0_ffn_W_up"]).T),
        W_gateT=np.ascontiguousarray(np.asarray(inputs["L0_ffn_W_gate"]).T),
        W_downT=np.ascontiguousarray(np.asarray(inputs["L0_ffn_W_down"]).T),
    )
    shared.update(consts)
    xs = np.asarray(inputs["L0_attn_hidden_states"])
    in_maps = [dict(shared, x=np.ascontiguousarray(xs[c])) for c in range(8)]
    res = run_bass_kernel_spmd(nc, in_maps, core_ids=list(range(8)))
    kernel.last_results = res.results
    return np.stack([np.asarray(res.results[c]["out"]) for c in range(8)], axis=0)


def build_program(nc, io, debug):
    with ExitStack() as stack:
        e = stack.enter_context
        h0 = e(nc.semaphore("ld_sy"))
        h1 = e(nc.semaphore("ld_gp"))
        h2 = e(nc.semaphore("ld_sc"))
        h3 = e(nc.semaphore("st_sy"))
        h4 = e(nc.semaphore("st_gp"))
        h5 = e(nc.semaphore("s_t"))
        h6 = e(nc.semaphore("s_v"))
        h7 = e(nc.semaphore("s_s"))
        h8 = e(nc.semaphore("s_g"))
        h9 = e(nc.semaphore("bar"))
        block = e(nc.Block())
        ctx = NS()
        ctx.nc, ctx.block = nc, block
        ctx.ld_sy, ctx.ld_gp, ctx.ld_sc = Sem(h0), Sem(h1), Sem(h2)
        ctx.st_sy, ctx.st_gp = Sem(h3), Sem(h4)
        ctx.s_t, ctx.s_v, ctx.s_s, ctx.s_g = Sem(h5), Sem(h6), Sem(h7), Sem(h8)
        ctx.bar_h = h9
        ctx.nbar = 0
        ctx.q_acc = e(nc.sbuf_tensor("q_acc", [128, 384], F32))
        ctx.q_scl = e(nc.sbuf_tensor("q_scl", [128, 16], F32))

        def barrier(exclude=()):
            ctx.nbar += 1
            tgt = 5 * ctx.nbar
            sems = [ctx.s_t, ctx.s_v, ctx.s_s, ctx.s_g,
                    ctx.ld_sy, ctx.ld_gp, ctx.ld_sc, ctx.st_sy, ctx.st_gp]
            vals = [(s.h, s.n) for s in sems if s.n > 0 and s not in exclude]
            for dec in (block.sync, block.tensor, block.vector, block.scalar, block.gpsimd):
                @dec
                def _(e):
                    for h, n in vals:
                        e.wait_ge(h, n)
                    e.sem_inc(ctx.bar_h, 1)
                    e.wait_ge(ctx.bar_h, tgt)
        ctx.barrier = barrier

        p2tiles = []
        for cc in range(8):
            for mj in (0, 1):
                for rt in range(16):
                    p2tiles.append((mj, rt, cc))
        for cc in range(2):
            for rt in range(64):
                p2tiles.append((2, rt, cc))
        ctx.p2i = [0]
        ctx.p2ev = {}
        qmats = [(io.W_upT, io.q_upT), (io.W_gateT, io.q_gateT), (io.W_downT, io.q_downT)]

        def emit_p2(count):
            b2, ev2 = block, ctx.p2ev
            scl = ctx.q_scl
            wt2, tmp2, qt2 = ctx.q_w2, ctx.q_tmp, ctx.q_qt
            for _ in range(count):
                i = ctx.p2i[0]
                if i >= len(p2tiles):
                    return
                mj, rt, cc = p2tiles[i]
                wr = qmats[mj][0].rearrange("(a p) c -> p a c", p=128)
                qr = qmats[mj][1].rearrange("(a p) c -> p a c", p=128)

                @b2.sync
                def _(s, i=i, rt=rt, cc=cc, wr=wr):
                    if i >= 2:
                        s.wait_ge(ctx.s_v.h, ev2[("q", i - 2)])
                    ctx.ld_sy.dma(s.dma_start(wt2[:, i % 2, :],
                                              wr[:, rt, cc * 1024:(cc + 1) * 1024]))
                ev2[("p2ld", i)] = ctx.ld_sy.n

                @b2.vector
                def _(v, i=i, mj=mj):
                    v.wait_ge(ctx.ld_sy.h, ev2[("p2ld", i)])
                    c0 = 3 * mj + 7
                    w = wt2[:, i % 2, :]
                    v.tensor_scalar(tmp2[:, :], w, scl[:, c0 + 1:c0 + 2], 1e30,
                                    ALU.add, ALU.mult)
                    v.tensor_scalar(tmp2[:, :], tmp2[:, :], scl[:, c0 + 2:c0 + 3], 0.0,
                                    ALU.max, ALU.min)
                    v.tensor_scalar(w, w, scl[:, c0 + 1:c0 + 2], 1e30,
                                    ALU.subtract, ALU.mult)
                    v.tensor_scalar(w, w, scl[:, c0:c0 + 1], 0.0,
                                    ALU.min, ALU.max)
                    if i >= 1:
                        v.wait_ge(ctx.st_gp.h, ev2[("qst", i - 1)])
                    ctx.s_v.inc(v.tensor_tensor(qt2[:, :], w, tmp2[:, :], ALU.add))
                ev2[("q", i)] = ctx.s_v.n

                @b2.gpsimd
                def _(g, i=i, rt=rt, cc=cc, qr=qr):
                    g.wait_ge(ctx.s_v.h, ev2[("q", i)])
                    ctx.st_gp.dma(g.dma_start(qr[:, rt, cc * 1024:(cc + 1) * 1024],
                                              qt2[:, :]))
                ev2[("qst", i)] = ctx.st_gp.n
                ctx.p2i[0] += 1
                if mj == 1 and rt == 15:
                    ev2[("qcc", cc)] = ctx.st_gp.n
                if ctx.p2i[0] == len(p2tiles):
                    ctx.ev_qdown = ctx.st_gp.n
        ctx.emit_p2 = emit_p2

        with ExitStack() as pstack:
            pe = pstack.enter_context
            sb = NS()
            sb.ident = pe(nc.sbuf_tensor("ident", [128, 128], BF16))
            sb.ut = pe(nc.sbuf_tensor("ut", [128, 2, CH], F32))
            sb.cw = pe(nc.sbuf_tensor("cw", [128, 34, KC], F32))
            sb.cb = pe(nc.sbuf_tensor("cb", [128, 34], F32))
            sb.dtb = pe(nc.sbuf_tensor("dtb_t", [128, H], F32))
            sb.a_t = pe(nc.sbuf_tensor("a_t", [128, H], F32))
            sb.d_t = pe(nc.sbuf_tensor("d_t", [128, H], F32))
            sb.dtraw = pe(nc.sbuf_tensor("dt_raw", [128, 32 * H], F32))
            sb.tiny = pe(nc.sbuf_tensor("tiny", [128, 64], F32))
            sb.wt1 = pe(nc.sbuf_tensor("q_w1", [128, 2, 1024], F32))

            p0_consts(ctx, io, sb)
            ctx.barrier()
            p1_ln(ctx, io, sb)
            ctx.barrier()
            if PHASES != "p1":
                with nc.sbuf_tensor("xnT", [128, 16, L], BF16) as sb_xnT:
                    sb.xnT = sb_xnT
                    p2_win(ctx, io, sb)
                    ctx.barrier(exclude=(ctx.ld_gp, ctx.s_g))
                if PHASES != "p2":
                    with ExitStack() as lstack:
                        le = lstack.enter_context
                        sb.nw = le(nc.sbuf_tensor("normw", [128, INTER], F32))
                        sb.expA = le(nc.sbuf_tensor("expA", [128, 32 * H], F32))
                        sb.expB = le(nc.sbuf_tensor("expB", [128, 32 * H], F32))
                        sb.erows = le(nc.sbuf_tensor("erows", [128, NCH * 192], F32))
                        sb.bt = le(nc.sbuf_tensor("btr", [128, L], BF16))
                        sb.ct = le(nc.sbuf_tensor("ctr", [128, L], BF16))
                        p3_conv(ctx, io, sb)
                        ctx.barrier(exclude=(ctx.ld_gp, ctx.s_g))
                        if PHASES != "p3":
                            p4_dt(ctx, io, sb)
                            ctx.barrier(exclude=(ctx.ld_gp, ctx.s_g))
                            if PHASES not in ("p4",):
                                p5_ssd(ctx, io, sb)
                                ctx.barrier(exclude=(ctx.ld_gp, ctx.s_g))
        if PHASES not in ("p1", "p2", "p3", "p4", "p5"):
            with ExitStack() as qstack:
                qe = qstack.enter_context
                ctx.q_w2 = qe(nc.sbuf_tensor("q_w2", [128, 2, 1024], F32))
                ctx.q_tmp = qe(nc.sbuf_tensor("q_tmp", [128, 1024], F32))
                ctx.q_qt = qe(nc.sbuf_tensor("q_qt", [128, 1024], BF16))
                ctx.q_idb = ctx.q_qt
                ctx.q_id32 = ctx.q_tmp
                p7_wout(ctx, io, sb)
                ctx.barrier()
                if PHASES in ("all", "p8", "p9"):
                    p9_upgate(ctx, io)
                    ctx.barrier()
                    if PHASES == "all":
                        p10_down(ctx, io)
        finish(ctx, io)


def finish(ctx, io):
    @ctx.block.gpsimd
    def _(g):
        g.wait_ge(ctx.st_gp.h, ctx.st_gp.n)

    @ctx.block.sync
    def _(s):
        s.wait_ge(ctx.st_gp.h, ctx.st_gp.n)
        s.wait_ge(ctx.st_sy.h, ctx.st_sy.n)


def p0_consts(ctx, io, sb):
    b = ctx.block
    ld, st = ctx.ld_sy, ctx.st_sy

    @b.sync
    def _(s):
        ld.dma(s.dma_start(sb.ident[:, :], io.c_ident[:, :]))
        ld.dma(s.dma_start(sb.ut[:, :, :], io.c_ut[:, :, :]))
        ld.dma(s.dma_start(sb.tiny[0:1, 0:H], io.A_log.rearrange("(o m) -> o m", o=1)))
        ld.dma(s.dma_start(sb.dtb[:, :],
                           io.dt_bias.rearrange("(o m) -> o m", o=1).partition_broadcast(128)))
        ld.dma(s.dma_start(sb.d_t[:, :],
                           io.D.rearrange("(o m) -> o m", o=1).partition_broadcast(128)))
        for t in range(34):
            ld.dma(s.dma_start(sb.cw[:, t, :], io.conv_wT[t * 128:(t + 1) * 128, :]))
            ld.dma(s.dma_start(
                sb.cb[:, t:t + 1],
                io.conv_b[t * 128:(t + 1) * 128].rearrange("(p o) -> p o", o=1)))
    ld_all = ld.n

    @b.scalar
    def _(sc):
        sc.wait_ge(ld.h, ld_all)
        ctx.s_s.inc(sc.activation(sb.tiny[0:1, 0:H], sb.tiny[0:1, 0:H], AF.Exp))
    ev_exp = ctx.s_s.n

    @b.vector
    def _(v):
        v.wait_ge(ctx.s_s.h, ev_exp)
        ctx.s_v.inc(v.tensor_scalar(sb.tiny[0:1, 0:H], sb.tiny[0:1, 0:H], -1.0, None, ALU.mult))
    ev_neg = ctx.s_v.n

    @b.sync
    def _(s):
        s.wait_ge(ctx.s_v.h, ev_neg)
        st.dma(s.dma_start(io.srow[0:1, 0:H], sb.tiny[0:1, 0:H]))
        s.wait_ge(st.h, st.n)
        ld.dma(s.dma_start(sb.a_t[:, :], io.srow[0:1, 0:H].partition_broadcast(128)))


def p1_ln(ctx, io, sb):
    nc, b = ctx.nc, ctx.block
    NT = 32
    with (
        nc.sbuf_tensor("p1_x", [128, 2, DM], F32) as xin,
        nc.sbuf_tensor("p1_sq", [128, 2, DM], F32) as sq,
        nc.sbuf_tensor("p1_xn", [128, 2, DM], BF16) as xnb,
        nc.sbuf_tensor("p1_cols", [128, 2, 8], F32) as cols,
    ):
        ld0, v0, s0, st0 = ctx.ld_sy.n, ctx.s_v.n, ctx.s_s.n, ctx.st_sy.n

        @b.sync
        def _(s):
            for i in range(NT):
                pp = i % 2
                if i >= 2:
                    s.wait_ge(ctx.s_s.h, s0 + 4 * (i - 1))
                ctx.ld_sy.dma(s.dma_start(xin[:, pp, :], io.x[i * 128:(i + 1) * 128, :]))
                if i >= 1:
                    s.wait_ge(ctx.s_s.h, s0 + 4 * i)
                    ctx.st_sy.dma(s.dma_start(io.xn_tok[(i - 1) * 128:i * 128, :],
                                              xnb[:, (i - 1) % 2, :]))
            s.wait_ge(ctx.s_s.h, s0 + 4 * NT)
            ctx.st_sy.dma(s.dma_start(io.xn_tok[(NT - 1) * 128:NT * 128, :],
                                      xnb[:, (NT - 1) % 2, :]))

        @b.vector
        def _(v):
            for i in range(NT):
                pp = i % 2
                v.wait_ge(ctx.ld_sy.h, ld0 + 16 * (i + 1))
                ctx.s_v.inc(v.tensor_reduce(cols[:, pp, 0:1], xin[:, pp, :], AXX, ALU.add))
                v.wait_ge(ctx.s_v.h, ctx.s_v.n)
                ctx.s_v.inc(v.tensor_scalar(cols[:, pp, 1:2], cols[:, pp, 0:1],
                                            -1.0 / DM, None, ALU.mult))
                v.wait_ge(ctx.s_s.h, s0 + 4 * i + 1)
                ctx.s_v.inc(v.tensor_reduce(cols[:, pp, 2:3], sq[:, pp, :], AXX, ALU.add))
                v.wait_ge(ctx.s_v.h, ctx.s_v.n)
                ctx.s_v.inc(v.tensor_scalar(cols[:, pp, 3:4], cols[:, pp, 2:3],
                                            1.0 / DM, 1e-5, ALU.mult, ALU.add))
                v.wait_ge(ctx.s_s.h, s0 + 4 * i + 3)
                ctx.s_v.inc(v.tensor_tensor(cols[:, pp, 6:7], cols[:, pp, 1:2],
                                            cols[:, pp, 5:6], ALU.mult))

        @b.scalar
        def _(sc):
            for i in range(NT):
                pp = i % 2
                sc.wait_ge(ctx.s_v.h, v0 + 5 * i + 2)
                ctx.s_s.inc(sc.activation(sq[:, pp, :], xin[:, pp, :], AF.Square,
                                          bias=cols[:, pp, 1:2], scale=1.0))
                sc.wait_ge(ctx.s_v.h, v0 + 5 * i + 4)
                ctx.s_s.inc(sc.activation(cols[:, pp, 4:5], cols[:, pp, 3:4], AF.Ln))
                sc.wait_ge(ctx.s_s.h, ctx.s_s.n)
                ctx.s_s.inc(sc.activation(cols[:, pp, 5:6], cols[:, pp, 4:5], AF.Exp, scale=-0.5))
                sc.wait_ge(ctx.s_v.h, v0 + 5 * i + 5)
                sc.wait_ge(ctx.s_s.h, ctx.s_s.n)
                if i >= 2:
                    sc.wait_ge(ctx.st_sy.h, st0 + 16 * (i - 1))
                ctx.s_s.inc(sc.activation(xnb[:, pp, :], xin[:, pp, :], AF.Identity,
                                          bias=cols[:, pp, 6:7], scale=cols[:, pp, 5:6]))


def p2_win(ctx, io, sb):
    nc, b = ctx.nc, ctx.block
    ld_sc0, ld_gp0 = ctx.ld_sc.n, ctx.ld_gp.n
    t0, v0, s0 = ctx.s_t.n, ctx.s_v.n, ctx.s_s.n
    st_sy0, st_gp0 = ctx.st_sy.n, ctx.st_gp.n

    @b.scalar
    def _(sc):
        for sl in range(8):
            for k in range(16):
                ctx.ld_sc.dma(sc.dma_start_transpose(
                    sb.xnT[:, k, sl * 512:(sl + 1) * 512],
                    io.xn_tok[sl * 512:(sl + 1) * 512, k * 128:(k + 1) * 128]))

    with (
        nc.sbuf_tensor("p2_wdt", [128, 16, H], BF16) as wdt,
        nc.sbuf_tensor("p2_wz", [128, 2, 16, 512], BF16) as wz,
        nc.sbuf_tensor("p2_wx", [128, 2, 16, 128], BF16) as wx,
        nc.sbuf_tensor("p2_zst", [128, 2, 512], F32) as zst,
        nc.sbuf_tensor("p2_xst", [128, 2, 512], F32) as xst,
        nc.psum_tensor("p2_psdt", [128, 2, 512], F32) as psdt,
        nc.psum_tensor("p2_psz", [128, 2, 512], F32) as psz,
        nc.psum_tensor("p2_psx", [128, 2, 512], F32) as psx,
    ):
        wiT = io.W_in.rearrange("(a p) c -> p a c", p=128)
        ev = {}

        @b.gpsimd
        def _(g):
            ctx.ld_gp.dma(g.dma_start(wdt[:, :, :], wiT[:, :, 8448:8512]))
            for nb in range(8):
                if nb >= 2:
                    g.wait_ge(ctx.s_t.h, t0 + 32 + 32 * (nb - 1))
                ctx.ld_gp.dma(g.dma_start(wz[:, nb % 2, :, :], wiT[:, :, nb * 512:(nb + 1) * 512]))

            def store_block(mb2):
                for nb in range(8):
                    idx = mb2 * 8 + nb
                    g.wait_ge(ctx.s_v.h, v0 + idx + 1)
                    ctx.st_gp.dma(g.dma_start(
                        io.xbc_cm[mb2 * 128:(mb2 + 1) * 128, nb * 512:(nb + 1) * 512],
                        xst[:, idx % 2, :]))

            for mb in range(34):
                if mb >= 2:
                    store_block(mb - 2)
                    g.wait_ge(ctx.s_t.h, t0 + 32 + 256 + 8 * (mb - 1))
                ctx.ld_gp.dma(g.dma_start(wx[:, mb % 2, :, :],
                                          wiT[:, :, 4096 + mb * 128:4096 + (mb + 1) * 128]))
            store_block(32)
            store_block(33)

        @b.tensor
        def _(t):
            t.wait_ge(ctx.ld_gp.h, ld_gp0 + 16)
            for m in range(32):
                t.wait_ge(ctx.ld_sc.h, ld_sc0 + 256 * (m // 4 + 1))
                if m >= 2:
                    t.wait_ge(ctx.s_s.h, s0 + (m - 1))
                mm = None
                for k in range(16):
                    mm = t.matmul(psdt[:, m % 2, 0:H], sb.xnT[:, k, m * 128:(m + 1) * 128],
                                  wdt[:, k, :], start=(k == 0), stop=(k == 15))
                ctx.s_t.inc(mm)
            for nb in range(8):
                t.wait_ge(ctx.ld_gp.h, ld_gp0 + 16 + 16 * (nb + 1))
                for m in range(32):
                    idx = nb * 32 + m
                    t.wait_ge(ctx.ld_sc.h, ld_sc0 + 256 * (m // 4 + 1))
                    if idx >= 2:
                        t.wait_ge(ctx.s_s.h, s0 + 32 + (idx - 1))
                    mm = None
                    for k in range(16):
                        mm = t.matmul(psz[:, idx % 2, :], sb.xnT[:, k, m * 128:(m + 1) * 128],
                                      wz[:, nb % 2, k, :], start=(k == 0), stop=(k == 15))
                    ctx.s_t.inc(mm)
            for mb in range(34):
                t.wait_ge(ctx.ld_gp.h, ld_gp0 + 16 + 128 + 16 * (mb + 1))
                for nb in range(8):
                    idx = mb * 8 + nb
                    if idx >= 2:
                        t.wait_ge(ctx.s_v.h, v0 + (idx - 1))
                    mm = None
                    for k in range(16):
                        mm = t.matmul(psx[:, idx % 2, :], wx[:, mb % 2, k, :],
                                      sb.xnT[:, k, nb * 512:(nb + 1) * 512],
                                      start=(k == 0), stop=(k == 15))
                    ctx.s_t.inc(mm)

        @b.scalar
        def _(sc):
            for m in range(32):
                sc.wait_ge(ctx.s_t.h, t0 + m + 1)
                ctx.s_s.inc(sc.copy(sb.dtraw[:, m * H:(m + 1) * H], psdt[:, m % 2, 0:H]))
            for nb in range(8):
                for m in range(32):
                    idx = nb * 32 + m
                    sc.wait_ge(ctx.s_t.h, t0 + 32 + idx + 1)
                    if idx >= 2:
                        sc.wait_ge(ctx.st_sy.h, st_sy0 + 16 * (idx - 1))
                    ctx.s_s.inc(sc.activation(zst[:, idx % 2, :], psz[:, idx % 2, :], AF.Silu))

        @b.sync
        def _(s):
            for nb in range(8):
                for m in range(32):
                    idx = nb * 32 + m
                    s.wait_ge(ctx.s_s.h, s0 + 32 + idx + 1)
                    ctx.st_sy.dma(s.dma_start(
                        io.sz_tok[m * 128:(m + 1) * 128, nb * 512:(nb + 1) * 512],
                        zst[:, idx % 2, :]))

        @b.vector
        def _(v):
            for mb in range(34):
                for nb in range(8):
                    idx = mb * 8 + nb
                    v.wait_ge(ctx.s_t.h, t0 + 32 + 256 + idx + 1)
                    if idx >= 2:
                        v.wait_ge(ctx.st_gp.h, st_gp0 + 16 * (idx - 1))
                    ctx.s_v.inc(v.tensor_copy(xst[:, idx % 2, :], psx[:, idx % 2, :]))

        sg0 = ctx.s_g.n

        p1tiles = []
        for wsrc in (io.W_upT, io.W_gateT, io.W_downT):
            wr1 = wsrc.rearrange("(a p) c -> p a c", p=128)
            for rt in range(wsrc.shape[0] // 128):
                for cc in range(wsrc.shape[1] // 1024):
                    p1tiles.append((wr1, rt, cc))
        assert len(p1tiles) == 384
        ld_gp_p1base = ctx.ld_gp.n
        ctx.p1i = [0]

        def emit_p1_load(i):
            wr1, rt, cc = p1tiles[i]

            @b.gpsimd
            def _(g, i=i, wr1=wr1, rt=rt, cc=cc):
                if i >= 2:
                    g.wait_ge(ctx.s_g.h, sg0 + i - 1)
                ctx.ld_gp.dma(g.dma_start(sb.wt1[:, i % 2, :],
                                          wr1[:, rt, cc * 1024:(cc + 1) * 1024]))
        emit_p1_load(0)

        def emit_p1(count):
            for _ in range(count):
                i = ctx.p1i[0]
                if i >= 384:
                    return
                if i + 1 < 384:
                    emit_p1_load(i + 1)

                @b.scalar
                def _(sc, i=i):
                    sc.wait_ge(ctx.ld_gp.h, ld_gp_p1base + 16 * (i + 1))
                    ctx.s_g.inc(sc.activation(sb.wt1[:, i % 2, :], sb.wt1[:, i % 2, :],
                                              AF.Abs, accum_out=ctx.q_acc[:, i:i + 1]))
                ctx.p1i[0] += 1
        ctx.emit_p1 = emit_p1
        ctx.ev_p1done = sg0 + 384
        emit_p1(96)

def p3_conv(ctx, io, sb):
    nc, b = ctx.nc, ctx.block
    ld0, s0, st0 = ctx.ld_sy.n, ctx.s_s.n, ctx.st_sy.n
    with (
        nc.sbuf_tensor("p3_pb", [128, 2, L + 3], F32) as pb,
        nc.sbuf_tensor("p3_oa", [128, 2, L], F32) as oa,
        nc.sbuf_tensor("p3_ob", [128, 2, L], F32) as ob,
        nc.sbuf_tensor("p3_xc", [128, 2, L], BF16) as xcs,
    ):
        @b.vector
        def _(v):
            v.memset(pb[:, 0, 0:3], 0.0)
            ctx.s_v.inc(v.memset(pb[:, 1, 0:3], 0.0))
        v0 = ctx.s_v.n

        @b.sync
        def _(s):
            ctx.ld_sy.dma(s.dma_start(
                sb.nw[:, :],
                io.norm_weight.rearrange("(o m) -> o m", o=1).partition_broadcast(128)))
            s.wait_ge(ctx.s_v.h, v0)
            for i in range(34):
                pp = i % 2
                if i >= 2:
                    s.wait_ge(ctx.s_v.h, v0 + (i - 1))
                ctx.ld_sy.dma(s.dma_start(pb[:, pp, 3:L + 3],
                                          io.xbc_cm[i * 128:(i + 1) * 128, :]))
                if i >= 2 and i - 2 < 32:
                    s.wait_ge(ctx.s_s.h, s0 + (i - 1))
                    ctx.st_sy.dma(s.dma_start(io.xc_cm[(i - 2) * 128:(i - 1) * 128, :],
                                              xcs[:, i % 2, :]))
            for i in (32, 33):
                if i - 2 < 32:
                    s.wait_ge(ctx.s_s.h, s0 + (i - 1))
                    ctx.st_sy.dma(s.dma_start(io.xc_cm[(i - 2) * 128:(i - 1) * 128, :],
                                              xcs[:, i % 2, :]))

        @b.vector
        def _(v):
            for i in range(34):
                pp = i % 2
                v.wait_ge(ctx.ld_sy.h, ld0 + 16 * (i + 2))
                if i >= 2:
                    v.wait_ge(ctx.s_s.h, s0 + (i - 1))
                v.tensor_scalar(oa[:, pp, :], pb[:, pp, 0:L],
                                sb.cw[:, i, 0:1], sb.cb[:, i:i + 1], ALU.mult, ALU.add)
                v.scalar_tensor_tensor(ob[:, pp, :], pb[:, pp, 1:L + 1],
                                       sb.cw[:, i, 1:2], oa[:, pp, :], ALU.mult, ALU.add)
                v.scalar_tensor_tensor(oa[:, pp, :], pb[:, pp, 2:L + 2],
                                       sb.cw[:, i, 2:3], ob[:, pp, :], ALU.mult, ALU.add)
                ctx.s_v.inc(v.scalar_tensor_tensor(ob[:, pp, :], pb[:, pp, 3:L + 3],
                                                   sb.cw[:, i, 3:4], oa[:, pp, :],
                                                   ALU.mult, ALU.add))

        for i in range(34):
            @b.scalar
            def _(sc, i=i):
                pp = i % 2
                sc.wait_ge(ctx.s_v.h, v0 + (i + 1))
                if i < 32:
                    if i >= 2:
                        sc.wait_ge(ctx.st_sy.h, st0 + 16 * (i - 1))
                    ctx.s_s.inc(sc.activation(xcs[:, pp, :], ob[:, pp, :], AF.Silu))
                elif i == 32:
                    ctx.s_s.inc(sc.activation(sb.bt[:, :], ob[:, pp, :], AF.Silu))
                else:
                    ctx.s_s.inc(sc.activation(sb.ct[:, :], ob[:, pp, :], AF.Silu))
            ctx.emit_p1(4)

def p4_dt(ctx, io, sb):
    nc, b = ctx.nc, ctx.block
    with (
        nc.sbuf_tensor("p4_du", [128, 32 * H], F32) as du,
        nc.sbuf_tensor("p4_t1", [128, 32 * H], F32) as t1,
        nc.sbuf_tensor("p4_t2", [128, 32 * H], F32) as t2,
        nc.sbuf_tensor("p4_dtc", [128, 32 * H], F32) as dtc,
        nc.sbuf_tensor("p4_ldt", [128, 32 * H], F32) as ldt,
        nc.sbuf_tensor("p4_da", [128, 32 * H], F32) as dA,
        nc.sbuf_tensor("p4_acs", [128, 32 * H], F32) as acs,
        nc.sbuf_tensor("p4_ab", [128, 32 * H], F32) as ab,
        nc.sbuf_tensor("p4_rows", [1, NCH * 192], F32) as rows,
        nc.sbuf_tensor("p4_rr", [128, NCH * 192], F32) as rrows,
        nc.psum_tensor("p4_ps", [128, 2, 512], F32) as ps,
    ):
        @b.vector
        def _(v):
            v.tensor_tensor(
                du.rearrange("p (t h) -> p t h", h=H)[:, :, :],
                sb.dtraw.rearrange("p (t h) -> p t h", h=H)[:, :, :],
                sb.dtb.rearrange("p (o h) -> p o h", o=1).broadcast_to([128, 32, H]), ALU.add)
            ctx.s_v.inc(v.tensor_scalar(t2[:, :], du[:, :], 0.0, None, ALU.max))
        ev_v1 = ctx.s_v.n

        @b.scalar
        def _(sc):
            sc.wait_ge(ctx.s_v.h, ev_v1)
            sc.activation(t1[:, :], du[:, :], AF.Abs)
            sc.activation(t1[:, :], t1[:, :], AF.Exp, scale=-1.0)
            ctx.s_s.inc(sc.activation(t1[:, :], t1[:, :], AF.Ln, bias=1.0))
        ev_s1 = ctx.s_s.n

        @b.vector
        def _(v):
            v.wait_ge(ctx.s_s.h, ev_s1)
            v.tensor_tensor(dtc[:, :], t2[:, :], t1[:, :], ALU.add)
            ctx.s_v.inc(v.tensor_scalar(dtc[:, :], dtc[:, :], 100.0, None, ALU.min))
        ev_v2 = ctx.s_v.n

        @b.scalar
        def _(sc):
            sc.wait_ge(ctx.s_v.h, ev_v2)
            ctx.s_s.inc(sc.activation(ldt[:, :], dtc[:, :], AF.Ln))
        ev_s2 = ctx.s_s.n

        @b.vector
        def _(v):
            ctx.s_v.inc(v.tensor_tensor(
                dA.rearrange("p (t h) -> p t h", h=H)[:, :, :],
                dtc.rearrange("p (t h) -> p t h", h=H)[:, :, :],
                sb.a_t.rearrange("p (o h) -> p o h", o=1).broadcast_to([128, 32, H]), ALU.mult))
        ev_v3 = ctx.s_v.n

        t_0 = ctx.s_t.n

        @b.tensor
        def _(t):
            t.wait_ge(ctx.s_v.h, ev_v3)
            for c in range(NCH):
                for mt in range(2):
                    idx = 2 * c + mt
                    if idx >= 2:
                        t.wait_ge(ctx.s_s.h, ev_s2 + (idx - 1))
                    mm = None
                    for kt in range(2):
                        mm = t.matmul(ps[:, idx % 2, 0:H],
                                      sb.ut[:, kt, mt * 128:(mt + 1) * 128],
                                      dA[:, (2 * c + kt) * H:(2 * c + kt + 1) * H],
                                      start=(kt == 0), stop=(kt == 1))
                    ctx.s_t.inc(mm)

        @b.scalar
        def _(sc):
            for idx in range(32):
                sc.wait_ge(ctx.s_t.h, t_0 + idx + 1)
                ctx.s_s.inc(sc.copy(acs[:, idx * H:(idx + 1) * H], ps[:, idx % 2, 0:H]))
        ev_s3 = ctx.s_s.n

        @b.sync
        def _(s):
            s.wait_ge(ctx.s_s.h, ev_s3)
            for c in range(NCH):
                ctx.st_sy.dma(s.dma_start(rows[0:1, c * 192:c * 192 + H],
                                          acs[0:1, (2 * c) * H:(2 * c + 1) * H]))
                ctx.st_sy.dma(s.dma_start(rows[0:1, c * 192 + H:c * 192 + 2 * H],
                                          acs[127:128, (2 * c + 1) * H:(2 * c + 2) * H]))
        ev_st1 = ctx.st_sy.n

        @b.vector
        def _(v):
            v.wait_ge(ctx.st_sy.h, ev_st1)
            r4 = rows.rearrange("o (c k h) -> o c k h", k=3, h=H)
            ctx.s_v.inc(v.tensor_tensor(r4[:, :, 2, :], r4[:, :, 1, :], r4[:, :, 0, :],
                                        ALU.subtract))
        ev_v4 = ctx.s_v.n

        @b.sync
        def _(s):
            s.wait_ge(ctx.s_v.h, ev_v4)
            ctx.st_sy.dma(s.dma_start(io.srow[1:2, 0:NCH * 192], rows[0:1, :]))
            s.wait_ge(ctx.st_sy.h, ctx.st_sy.n)
            ctx.ld_sy.dma(s.dma_start(rrows[:, :], io.srow[1:2, 0:NCH * 192].partition_broadcast(128)))
        ev_ld1 = ctx.ld_sy.n

        @b.vector
        def _(v):
            v.wait_ge(ctx.ld_sy.h, ev_ld1)
            mm = None
            for c in range(NCH):
                mm = v.tensor_tensor(
                    ab.rearrange("p (t h) -> p t h", h=H)[:, 2 * c:2 * c + 2, :],
                    acs.rearrange("p (t h) -> p t h", h=H)[:, 2 * c:2 * c + 2, :],
                    rrows[:, c * 192:c * 192 + H].rearrange("p (o h) -> p o h", o=1)
                    .broadcast_to([128, 2, H]), ALU.subtract)
            ctx.s_v.inc(mm)
            v.wait_ge(ctx.s_v.h, ctx.s_v.n)
            v.tensor_tensor(ldt[:, :], ldt[:, :], ab[:, :], ALU.subtract)
            ctx.s_v.inc(v.tensor_scalar(ldt[:, :], ldt[:, :], 80.0, None, ALU.min))
        ev_v5 = ctx.s_v.n

        @b.scalar
        def _(sc):
            sc.wait_ge(ctx.s_v.h, ev_v5)
            sc.activation(sb.expA[:, :], ab[:, :], AF.Exp)
            sc.activation(sb.expB[:, :], ldt[:, :], AF.Exp)
            ctx.s_s.inc(sc.activation(sb.erows[:, :], rrows[:, :], AF.Exp))
        ev_s4 = ctx.s_s.n

        if io.dbg is not None:
            @b.sync
            def _(s):
                s.wait_ge(ctx.s_s.h, ev_s4)
                ctx.st_sy.dma(s.dma_start(io.dbg[:, 0:2048], dtc[:, :]))
                ctx.st_sy.dma(s.dma_start(io.dbg[:, 2048:4096], acs[:, :]))
                ctx.st_sy.dma(s.dma_start(io.dbg[:, 4096:6144], sb.expA[:, :]))
                ctx.st_sy.dma(s.dma_start(io.dbg[:, 6144:8192], sb.expB[:, :]))
                ctx.st_sy.dma(s.dma_start(io.dbg[:, 8192:8192 + 3072], sb.erows[:, :]))

def p5_ssd(ctx, io, sb):
    nc, b = ctx.nc, ctx.block
    with (
        nc.sbuf_tensor("p5_x2", [128, 2, INTER], BF16) as x2,
        nc.sbuf_tensor("p5_xb", [128, 2, INTER], BF16) as xb,
        nc.sbuf_tensor("p5_cbt", [128, 2, 2, CH], BF16) as cbt,
        nc.sbuf_tensor("p5_btok", [128, 2, 2, 128], BF16) as btok,
        nc.sbuf_tensor("p5_sA", [128, INTER], F32) as sA,
        nc.sbuf_tensor("p5_sB", [128, INTER], F32) as sB,
        nc.sbuf_tensor("p5_ssc", [128, INTER], BF16) as ssc,
        nc.sbuf_tensor("p5_yb", [128, INTER], F32) as yb,
        nc.sbuf_tensor("p5_sz", [128, 2048], F32) as szb,
        nc.sbuf_tensor("p5_ts", [128, INTER], F32) as tsc,
        nc.sbuf_tensor("p5_yfb", [128, INTER], BF16) as yfb,
        nc.sbuf_tensor("p5_cols", [128, 16], F32) as cols,
        nc.psum_tensor("p5_pst", [128, 4, 512], F32) as pst,
        nc.psum_tensor("p5_psy", [128, 2, 512], F32) as psy,
        nc.psum_tensor("p5_psc", [128, 512], F32) as psc,
        nc.psum_tensor("p5_psb", [128, 1024], BF16) as psb,
    ):
        ev = {}

        @b.vector
        def _(v):
            v.memset(sA[:, :], 0.0)
            ctx.s_v.inc(v.memset(cols[:, 8:9], 1e-8))
        ev[("sinit",)] = ctx.s_v.n
        yidx = [0]

        def er_slice(c, kind, h0, nh):
            off = c * 192 + {"r": 0, "ce": H, "cer": 2 * H}[kind] + h0
            return sb.erows[:, off:off + nh]

        for c in range(NCH):
            S_in = sA if c % 2 == 0 else sB
            S_out = sB if c % 2 == 0 else sA
            pp = c % 2

            @b.scalar
            def _(sc, c=c):
                if c >= 1:
                    sc.wait_ge(ctx.s_v.h, ev[("dterm", c - 1, 1)])
                for lt in range(2):
                    ctx.ld_sc.dma(sc.dma_start_transpose(
                        x2[:, lt, :], io.xc_cm[:, (2 * c + lt) * 128:(2 * c + lt + 1) * 128]))
            ev[("x2", c)] = ctx.ld_sc.n

            @b.tensor
            def _(t, c=c, pp=pp):
                if c >= 1:
                    t.wait_ge(ctx.s_v.h, ev[("cbtev", c - 1)])
                mm = None
                for st in range(2):
                    mm = t.matmul(psc[:, st * 256:(st + 1) * 256],
                                  sb.bt[:, c * 256 + st * 128:c * 256 + (st + 1) * 128],
                                  sb.ct[:, c * 256:(c + 1) * 256], start=True, stop=True)
                ctx.s_t.inc(mm)
                if c >= 1:
                    t.wait_ge(ctx.s_v.h, ev[("btokev", c - 1)])
                mm = None
                for st in range(2):
                    mm = t.transpose(psb[:, st * 128:(st + 1) * 128],
                                     sb.bt[:, c * 256 + st * 128:c * 256 + (st + 1) * 128],
                                     sb.ident[:, :])
                ctx.s_t.inc(mm)
            ev[("cbtmm", c)] = ctx.s_t.n - 1
            ev[("btokmm", c)] = ctx.s_t.n

            @b.vector
            def _(v, c=c, pp=pp, S_in=S_in):
                v.wait_ge(ctx.s_t.h, ev[("cbtmm", c)])
                mm = None
                for st in range(2):
                    mm = v.tensor_tensor(cbt[:, pp, st, :], psc[:, st * 256:(st + 1) * 256],
                                         sb.ut[:, st, :], ALU.mult)
                ctx.s_v.inc(mm)
                ev[("cbtev", c)] = ctx.s_v.n
                v.wait_ge(ctx.s_t.h, ev[("btokmm", c)])
                for st in range(2):
                    mm = v.tensor_copy(btok[:, pp, st, :], psb[:, st * 128:(st + 1) * 128])
                ctx.s_v.inc(mm)
                ev[("btokev", c)] = ctx.s_v.n
                for half in range(2):
                    mm = v.tensor_tensor(
                        ssc.rearrange("p (h q) -> p h q", q=PD)[:, half * 32:(half + 1) * 32, :],
                        S_in.rearrange("p (h q) -> p h q", q=PD)[:, half * 32:(half + 1) * 32, :],
                        er_slice(c, "r", half * 32, 32).rearrange("p (h o) -> p h o", o=1)
                        .broadcast_to([128, 32, PD]), ALU.mult)
                ctx.s_v.inc(mm)
                ev[("ssc", c)] = ctx.s_v.n
                v.wait_ge(ctx.ld_sc.h, ev[("x2", c)])
                for lt in range(2):
                    mm = v.tensor_tensor(
                        xb.rearrange("p s (h q) -> p s h q", q=PD)[:, lt, :, :],
                        x2.rearrange("p s (h q) -> p s h q", q=PD)[:, lt, :, :],
                        sb.expB[:, (2 * c + lt) * H:(2 * c + lt + 1) * H]
                        .rearrange("p (h o) -> p h o", o=1).broadcast_to([128, H, PD]),
                        ALU.mult)
                ctx.s_v.inc(mm)
                ev[("xb", c)] = ctx.s_v.n

            for half in range(2):
                @b.tensor
                def _(t, c=c, pp=pp, half=half):
                    if half == 0:
                        t.wait_ge(ctx.s_v.h, ev[("xb", c)])
                        if c >= 1:
                            t.wait_ge(ctx.s_v.h, ev[("supd", c - 1, 1)])
                    else:
                        t.wait_ge(ctx.s_v.h, ev[("supd", c, 0)])
                    mm = None
                    for j in range(4):
                        for st in range(2):
                            mm = t.matmul(pst[:, j, :], btok[:, pp, st, :],
                                          xb[:, st, half * 2048 + j * 512:half * 2048 + (j + 1) * 512],
                                          start=(st == 0), stop=(st == 1))
                    ctx.s_t.inc(mm)
                    ev[("states", c, half)] = ctx.s_t.n

                @b.vector
                def _(v, c=c, S_in=S_in, S_out=S_out, half=half):
                    if c >= 1 and half == 0:
                        v.wait_ge(ctx.s_s.h, ev[("yn", c - 1, 1)])
                    v.wait_ge(ctx.s_t.h, ev[("states", c, half)])
                    sl = slice(half * 2048, (half + 1) * 2048)
                    h0 = half * 32
                    pst_f = pst.rearrange("p j n -> p (j n)")
                    v.tensor_tensor(
                        tsc.rearrange("p (h q) -> p h q", q=PD)[:, h0:h0 + 32, :],
                        pst_f.rearrange("p (h q) -> p h q", q=PD)[:, 0:32, :],
                        er_slice(c, "cer", h0, 32).rearrange("p (h o) -> p h o", o=1)
                        .broadcast_to([128, 32, PD]), ALU.mult)
                    v.tensor_tensor(
                        S_out.rearrange("p (h q) -> p h q", q=PD)[:, h0:h0 + 32, :],
                        S_in.rearrange("p (h q) -> p h q", q=PD)[:, h0:h0 + 32, :],
                        er_slice(c, "ce", h0, 32).rearrange("p (h o) -> p h o", o=1)
                        .broadcast_to([128, 32, PD]), ALU.mult)
                    ctx.s_v.inc(v.tensor_tensor(S_out[:, sl], S_out[:, sl], tsc[:, sl], ALU.add))
                    ev[("supd", c, half)] = ctx.s_v.n

            for lt in range(2):
                @b.sync
                def _(s, c=c, lt=lt):
                    if (c, lt) != (0, 0):
                        pc, pl = (c, 0) if lt == 1 else (c - 1, 1)
                        s.wait_ge(ctx.s_v.h, ev[("gate", pc, pl)])
                    ctx.ld_sy.dma(s.dma_start(
                        szb[:, :], io.sz_tok[(2 * c + lt) * 128:(2 * c + lt + 1) * 128, 0:2048]))
                ev[("szld", c, lt)] = ctx.ld_sy.n

                for hg in range(8):
                    yi = yidx[0]

                    @b.tensor
                    def _(t, c=c, lt=lt, pp=pp, hg=hg, yi=yi):
                        if hg == 0:
                            t.wait_ge(ctx.s_v.h, ev[("ssc", c)])
                        if yi >= 2:
                            t.wait_ge(ctx.s_v.h, ev[("comb", yi - 2)])
                        bank = yi % 2
                        for st in range(2):
                            t.matmul(psy[:, bank, :], cbt[:, pp, st, lt * 128:(lt + 1) * 128],
                                     xb[:, st, hg * 512:(hg + 1) * 512],
                                     start=(st == 0), stop=False)
                        mm = t.matmul(psy[:, bank, :],
                                      sb.ct[:, c * 256 + lt * 128:c * 256 + (lt + 1) * 128],
                                      ssc[:, hg * 512:(hg + 1) * 512], start=False, stop=True)
                        ctx.s_t.inc(mm)
                        ev[("ymm", yi)] = ctx.s_t.n

                    @b.vector
                    def _(v, c=c, lt=lt, hg=hg, yi=yi):
                        if lt == 1 and hg == 0:
                            v.wait_ge(ctx.s_s.h, ev[("yn", c, 0)])
                        v.wait_ge(ctx.s_t.h, ev[("ymm", yi)])
                        ctx.s_v.inc(v.tensor_tensor(
                            yb.rearrange("p (h q) -> p h q", q=PD)[:, hg * 8:(hg + 1) * 8, :],
                            psy[:, yi % 2, :].rearrange("p (h q) -> p h q", q=PD)[:, :, :],
                            sb.expA[:, (2 * c + lt) * H + hg * 8:(2 * c + lt) * H + (hg + 1) * 8]
                            .rearrange("p (h o) -> p h o", o=1).broadcast_to([128, 8, PD]),
                            ALU.mult))
                        ev[("comb", yi)] = ctx.s_v.n
                    yidx[0] += 1

                @b.vector
                def _(v, c=c, lt=lt):
                    if lt == 1:
                        v.wait_ge(ctx.s_s.h, ev[("yn", c, 0)])
                    v.tensor_tensor(
                        tsc.rearrange("p (h q) -> p h q", q=PD)[:, :, :],
                        x2.rearrange("p s (h q) -> p s h q", q=PD)[:, lt, :, :],
                        sb.d_t.rearrange("p (h o) -> p h o", o=1).broadcast_to([128, H, PD]),
                        ALU.mult)
                    ctx.s_v.inc(v.tensor_tensor(yb[:, :], yb[:, :], tsc[:, :], ALU.add))
                    ev[("dterm", c, lt)] = ctx.s_v.n
                    v.wait_ge(ctx.ld_sy.h, ev[("szld", c, lt)])
                    ctx.s_v.inc(v.tensor_tensor(tsc[:, 0:2048], yb[:, 0:2048], szb[:, :],
                                                ALU.mult))
                    ev[("gate0", c, lt)] = ctx.s_v.n

                @b.sync
                def _(s, c=c, lt=lt):
                    s.wait_ge(ctx.s_v.h, ev[("gate0", c, lt)])
                    ctx.ld_sy.dma(s.dma_start(
                        szb[:, :],
                        io.sz_tok[(2 * c + lt) * 128:(2 * c + lt + 1) * 128, 2048:4096]))
                ev[("szld1", c, lt)] = ctx.ld_sy.n

                @b.vector
                def _(v, c=c, lt=lt):
                    v.wait_ge(ctx.ld_sy.h, ev[("szld1", c, lt)])
                    ctx.s_v.inc(v.tensor_tensor(tsc[:, 2048:4096], yb[:, 2048:4096], szb[:, :],
                                                ALU.mult))
                    ev[("gate", c, lt)] = ctx.s_v.n

                @b.scalar
                def _(sc, c=c, lt=lt):
                    sc.wait_ge(ctx.s_v.h, ev[("gate", c, lt)])
                    ctx.s_s.inc(sc.activation(yb[:, :], tsc[:, :], AF.Square,
                                              accum_out=cols[:, 0 + lt:1 + lt]))
                    sc.wait_ge(ctx.s_s.h, ctx.s_s.n)
                    ctx.s_s.inc(sc.activation(cols[:, 2 + lt:3 + lt], cols[:, 0 + lt:1 + lt],
                                              AF.Identity, scale=1.0 / INTER,
                                              bias=cols[:, 8:9]))
                    sc.wait_ge(ctx.s_s.h, ctx.s_s.n)
                    ctx.s_s.inc(sc.activation(cols[:, 4 + lt:5 + lt], cols[:, 2 + lt:3 + lt],
                                              AF.Ln))
                    sc.wait_ge(ctx.s_s.h, ctx.s_s.n)
                    ctx.s_s.inc(sc.activation(cols[:, 6 + lt:7 + lt], cols[:, 4 + lt:5 + lt],
                                              AF.Exp, scale=-0.5))
                    sc.wait_ge(ctx.s_s.h, ctx.s_s.n)
                    ctx.s_s.inc(sc.activation(yb[:, :], tsc[:, :], AF.Identity,
                                              scale=cols[:, 6 + lt:7 + lt]))
                ev[("yn", c, lt)] = ctx.s_s.n

                @b.vector
                def _(v, c=c, lt=lt):
                    v.wait_ge(ctx.s_s.h, ev[("yn", c, lt)])
                    if (c, lt) != (0, 0):
                        pc, pl = (c, 0) if lt == 1 else (c - 1, 1)
                        v.wait_ge(ctx.st_sy.h, ev[("yfst", pc, pl)])
                    ctx.s_v.inc(v.tensor_tensor(yfb[:, :], yb[:, :], sb.nw[:, :], ALU.mult))
                ev[("yfb", c, lt)] = ctx.s_v.n

                @b.sync
                def _(s, c=c, lt=lt):
                    s.wait_ge(ctx.s_v.h, ev[("yfb", c, lt)])
                    ctx.st_sy.dma(s.dma_start(
                        io.yf_tok[(2 * c + lt) * 128:(2 * c + lt + 1) * 128, :], yfb[:, :]))
                ev[("yfst", c, lt)] = ctx.st_sy.n
                ctx.emit_p1(5)


def p7_wout(ctx, io, sb):
    nc, b = ctx.nc, ctx.block
    with (
        nc.sbuf_tensor("p7_w", [128, 32, 1024], BF16) as wsb,
        nc.sbuf_tensor("p7_yf", [128, 2, 32, 512], BF16) as yfT,
        nc.sbuf_tensor("p7_st", [128, 2, 1024], F32) as stg,
        nc.psum_tensor("p7_ps", [128, 4, 512], F32) as ps,
    ):
        ev = {}
        eidx = [0]

        scl = ctx.q_scl

        @b.sync
        def _(s):
            ctx.ld_sy.dma(s.dma_start(ctx.q_idb[:, 0:128], io.c_ident[:, :]))
        ev["idld"] = ctx.ld_sy.n

        @b.scalar
        def _(sc):
            sc.wait_ge(ctx.ld_sy.h, ev["idld"])
            ctx.s_s.inc(sc.copy(ctx.q_id32[:, 0:128], ctx.q_idb[:, 0:128]))
        ev["id32"] = ctx.s_s.n

        @b.vector
        def _(v):
            v.wait_ge(ctx.s_g.h, ctx.ev_p1done)
            mm = None
            for j in range(3):
                mm = v.tensor_reduce(scl[:, j:j + 1], ctx.q_acc[:, j * 128:(j + 1) * 128],
                                     AXX, ALU.add)
            ctx.s_v.inc(mm)
        ev["accred"] = ctx.s_v.n

        @b.tensor
        def _(t):
            t.wait_ge(ctx.s_v.h, ev["accred"])
            t.wait_ge(ctx.s_s.h, ev["id32"])
            psf = ps.rearrange("p a n -> p (a n)")
            ctx.s_t.inc(t.transpose(psf[0:4, 0:128], scl[:, 0:4], ctx.q_id32[:, 0:128]))
        ev["acstr"] = ctx.s_t.n
        ctx.ev_id32done = ev["id32"]
        ctx.ev_acstr = ev["acstr"]

        @b.vector
        def _(v):
            v.wait_ge(ctx.s_t.h, ev["acstr"])
            psf = ps.rearrange("p a n -> p (a n)")
            ctx.s_v.inc(v.tensor_reduce(scl[0:4, 4:5], psf[0:4, 0:128], AXX, ALU.add))
            v.wait_ge(ctx.s_v.h, ctx.s_v.n)
            ctx.s_v.inc(v.tensor_scalar(scl[0:4, 5:6], scl[0:4, 4:5], 1.0 / (DM * DE), 1e-8,
                                        ALU.mult, ALU.add))
            v.wait_ge(ctx.s_v.h, ctx.s_v.n)
            v.tensor_scalar(scl[0:4, 6:7], scl[0:4, 5:6], 0.5, None, ALU.mult)
            ctx.s_v.inc(v.tensor_scalar(scl[0:4, 7:8], scl[0:4, 5:6], -1.0, None, ALU.mult))
        ev["sclv"] = ctx.s_v.n

        @b.sync
        def _(s):
            s.wait_ge(ctx.s_v.h, ev["sclv"])
            ctx.st_sy.dma(s.dma_start(io.srow[2:6, 0:3], scl[0:4, 5:8]))
            s.wait_ge(ctx.st_sy.h, ctx.st_sy.n)
            for j in range(3):
                ctx.ld_sy.dma(s.dma_start(
                    scl[:, 3 * j + 7:3 * j + 10],
                    io.srow[2 + j:3 + j, 0:3].partition_broadcast(128)))
        ev["finread"] = ctx.s_v.n

        for half in range(2):
            @b.gpsimd
            def _(g, half=half):
                if half == 1:
                    g.wait_ge(ctx.s_t.h, ev[("mmall", 0)])
                for q in range(4):
                    ctx.ld_gp.dma(g.dma_start(
                        wsb[:, q * 8:(q + 1) * 8, :],
                        io.W_out.rearrange("(a p) d -> p a d", p=128)
                        [:, q * 8:(q + 1) * 8, half * 1024:(half + 1) * 1024]))
            ev[("wld", half)] = ctx.ld_gp.n

            for sl in range(8):
                @b.scalar
                def _(sc, half=half, sl=sl):
                    if half * 8 + sl >= 2:
                        sc.wait_ge(ctx.s_t.h, ev[("slmm", half * 8 + sl - 2)])
                    for k in range(32):
                        ctx.ld_sc.dma(sc.dma_start_transpose(
                            yfT[:, sl % 2, k, :],
                            io.yf_tok[sl * 512:(sl + 1) * 512, k * 128:(k + 1) * 128]))
                ev[("yfld", half, sl)] = ctx.ld_sc.n

                for tt in range(4):
                    ttg = (half * 8 + sl) * 4 + tt
                    for n in range(2):
                        e = eidx[0]

                        @b.tensor
                        def _(t, half=half, sl=sl, tt=tt, n=n, e=e):
                            if e % 8 == 0:
                                t.wait_ge(ctx.ld_sc.h, ev[("yfld", half, sl)])
                                t.wait_ge(ctx.ld_gp.h, ev[("wld", half)])
                            if e == 0:
                                t.wait_ge(ctx.s_v.h, ev["sclv"])
                            if e >= 4:
                                t.wait_ge(ctx.s_v.h, ev[("ev", e - 4)])
                            mm = None
                            for k in range(32):
                                mm = t.matmul(ps[:, e % 4, :],
                                              yfT[:, sl % 2, k, tt * 128:(tt + 1) * 128],
                                              wsb[:, k, n * 512:(n + 1) * 512],
                                              start=(k == 0), stop=(k == 31))
                            ctx.s_t.inc(mm)
                            ev[("mm", e)] = ctx.s_t.n

                        @b.vector
                        def _(v, ttg=ttg, n=n, e=e):
                            v.wait_ge(ctx.s_t.h, ev[("mm", e)])
                            if ttg >= 2:
                                v.wait_ge(ctx.st_gp.h, ev[("sst", ttg - 2)])
                            ctx.s_v.inc(v.tensor_copy(stg[:, ttg % 2, n * 512:(n + 1) * 512],
                                                      ps[:, e % 4, :]))
                            ev[("ev", e)] = ctx.s_v.n
                        eidx[0] += 1

                    @b.gpsimd
                    def _(g, half=half, sl=sl, tt=tt, ttg=ttg):
                        e = eidx[0] - 1
                        g.wait_ge(ctx.s_v.h, ev[("ev", e)])
                        tti = sl * 4 + tt
                        ctx.st_gp.dma(g.dma_start(
                            io.s_tok[tti * 128:(tti + 1) * 128, half * 1024:(half + 1) * 1024],
                            stg[:, ttg % 2, :]))
                        ev[("sst", ttg)] = ctx.st_gp.n
                    ctx.emit_p2(2)
                ev[("slmm", half * 8 + sl)] = ctx.s_t.n
            ev[("mmall", half)] = ctx.s_t.n


def p9_upgate(ctx, io):
    nc, b = ctx.nc, ctx.block
    with (
        nc.sbuf_tensor("p9_xnT", [128, 16, L], BF16) as xnT,
        nc.sbuf_tensor("p9_wq", [128, 2, 2, 8, 512], BF16) as wq,
        nc.sbuf_tensor("p9_hu", [128, 2, 512], F32) as hu,
        nc.sbuf_tensor("p9_hb", [128, 2, 512], BF16) as hb,
        nc.psum_tensor("p9_ps", [128, 4, 512], F32) as ps,
    ):
        ev = {}

        @b.scalar
        def _(sc):
            for sl in range(8):
                for k in range(16):
                    ctx.ld_sc.dma(sc.dma_start_transpose(
                        xnT[:, k, sl * 512:(sl + 1) * 512],
                        io.xn_tok[sl * 512:(sl + 1) * 512, k * 128:(k + 1) * 128]))
        ev["xnTsc"] = ctx.ld_sc.n

        qsrc = [io.q_upT, io.q_gateT]
        ei = [0]
        for eb in range(16):
            for sub in range(2):
                @b.sync
                def _(s, eb=eb, sub=sub):
                    if eb >= 1:
                        s.wait_ge(ctx.s_t.h, ev[("ebmm", eb - 1)])
                    s.wait_ge(ctx.st_gp.h, ctx.p2ev[("qcc", eb // 2)])
                    for mat in range(2):
                        for k in range(8):
                            ctx.ld_sy.dma(s.dma_start(
                                wq[:, sub, mat, k, :],
                                qsrc[mat][(sub * 8 + k) * 128:(sub * 8 + k + 1) * 128,
                                          eb * 512:(eb + 1) * 512]))
                ev[("qld", eb, sub)] = ctx.ld_sy.n

            waited1 = [False]
            for m in range(4):
                for n in range(8):
                    e = ei[0]

                    @b.tensor
                    def _(t, eb=eb, m=m, n=n, e=e):
                        if e % 32 == 0:
                            t.wait_ge(ctx.ld_sy.h, ev[("qld", eb, 0)])
                        if eb == 0 and e == 0:
                            t.wait_ge(ctx.ld_sc.h, ev["xnTsc"])
                        pp2 = e % 2
                        if e >= 2:
                            t.wait_ge(ctx.s_v.h, ev[("hev", e - 2)])
                        mm = None
                        for mat in range(2):
                            for k in range(16):
                                if k == 8 and not waited1[0]:
                                    t.wait_ge(ctx.ld_sy.h, ev[("qld", eb, 1)])
                                    waited1[0] = True
                                mm = t.matmul(ps[:, 2 * pp2 + mat, :],
                                              wq[:, k // 8, mat, k % 8,
                                                 m * 128:(m + 1) * 128],
                                              xnT[:, k, n * 512:(n + 1) * 512],
                                              start=(k == 0), stop=(k == 15))
                        ctx.s_t.inc(mm)
                        ev[("mmup", e)] = ctx.s_t.n

                    @b.scalar
                    def _(sc, e=e):
                        sc.wait_ge(ctx.s_t.h, ev[("mmup", e)])
                        if e >= 2:
                            sc.wait_ge(ctx.s_v.h, ev[("hev", e - 2)])
                        ctx.s_s.inc(sc.activation(hu[:, e % 2, :], ps[:, 2 * (e % 2), :],
                                                  AF.Silu))
                        ev[("silu", e)] = ctx.s_s.n

                    @b.vector
                    def _(v, e=e):
                        v.wait_ge(ctx.s_s.h, ev[("silu", e)])
                        if e >= 2:
                            v.wait_ge(ctx.st_gp.h, ev[("hst", e - 2)])
                        ctx.s_v.inc(v.tensor_tensor(hb[:, e % 2, :], hu[:, e % 2, :],
                                                    ps[:, 2 * (e % 2) + 1, :], ALU.mult))
                        ev[("hev", e)] = ctx.s_v.n

                    @b.gpsimd
                    def _(g, eb=eb, m=m, n=n, e=e):
                        g.wait_ge(ctx.s_v.h, ev[("hev", e)])
                        ctx.st_gp.dma(g.dma_start(
                            io.hid_cm[eb * 512 + m * 128:eb * 512 + (m + 1) * 128,
                                      n * 512:(n + 1) * 512],
                            hb[:, e % 2, :]))
                        ev[("hst", e)] = ctx.st_gp.n
                    ei[0] += 1
                    if (m * 8 + n) % 2 == 1:
                        ctx.emit_p2(1)
            ev[("ebmm", eb)] = ctx.s_t.n
        ctx.emit_p2(400)


def p10_down(ctx, io):
    nc, b = ctx.nc, ctx.block
    with (
        nc.sbuf_tensor("p10_qd", [128, 64, 2, 512], BF16) as qd,
        nc.sbuf_tensor("p10_h", [128, 2, 32, 256], BF16) as hd,
        nc.sbuf_tensor("p10_g", [128, 4, 512], F32) as gb,
        nc.sbuf_tensor("p10_s", [128, 4, 512], F32) as sxs,
        nc.sbuf_tensor("p10_x", [128, 4, 512], F32) as sxx,
        nc.sbuf_tensor("p10_o", [128, 4, 512], F32) as ob,
        nc.psum_tensor("p10_ps", [128, 8, 512], F32) as ps,
    ):
        ev = {}
        for half in range(2):
            @b.sync
            def _(s, half=half):
                if half == 0:
                    s.wait_ge(ctx.st_gp.h, ctx.ev_qdown)
                if half == 1:
                    s.wait_ge(ctx.s_t.h, ev[("tpmm", 15)])
                for ek in range(64):
                    for n in range(2):
                        ctx.ld_sy.dma(s.dma_start(
                            qd[:, ek, n, :],
                            io.q_downT[ek * 128:(ek + 1) * 128,
                                       half * 1024 + n * 512:half * 1024 + (n + 1) * 512]))
            ev[("qdld", half)] = ctx.ld_sy.n

            hid_r = io.hid_cm.rearrange("(a p) t -> p a t", p=128)
            for tp in range(16):
                idx = half * 16 + tp
                pp = idx % 2

                @b.gpsimd
                def _(g, half=half, tp=tp, idx=idx):
                    if idx >= 1:
                        g.wait_ge(ctx.s_t.h, ev[("tpmm_h", idx - 1)])
                    ctx.ld_gp.dma(g.dma_start(
                        hd[:, 0, :, :], hid_r[:, 0:32, tp * 256:(tp + 1) * 256]))
                    ev[("hld", idx, 0)] = ctx.ld_gp.n
                    if idx >= 1:
                        g.wait_ge(ctx.s_t.h, ev[("tpmm", idx - 1)])
                    ctx.ld_gp.dma(g.dma_start(
                        hd[:, 1, :, :], hid_r[:, 32:64, tp * 256:(tp + 1) * 256]))
                    ev[("hld", idx, 1)] = ctx.ld_gp.n

                @b.sync
                def _(s, half=half, tp=tp, idx=idx):
                    if idx >= 1:
                        s.wait_ge(ctx.s_v.h, ev[("oev", idx - 1)])
                    for ttj in range(2):
                        tti = tp * 2 + ttj
                        for n in range(2):
                            c0 = half * 1024 + n * 512
                            ctx.ld_sy.dma(s.dma_start(
                                sxs[:, 2 * ttj + n, :],
                                io.s_tok[tti * 128:(tti + 1) * 128, c0:c0 + 512]))
                            ctx.ld_sy.dma(s.dma_start(
                                sxx[:, 2 * ttj + n, :],
                                io.x[tti * 128:(tti + 1) * 128, c0:c0 + 512]))
                ev[("sld", idx)] = ctx.ld_sy.n

                @b.tensor
                def _(t, half=half, tp=tp, idx=idx, pp=pp):
                    t.wait_ge(ctx.ld_gp.h, ev[("hld", idx, 0)])
                    t.wait_ge(ctx.ld_sy.h, ev[("qdld", half)])
                    if idx >= 2:
                        t.wait_ge(ctx.s_v.h, ev[("oev", idx - 2)])
                    mm = None
                    for ek in range(32):
                        for ttj in range(2):
                            for n in range(2):
                                mm = t.matmul(ps[:, 4 * pp + 2 * ttj + n, :],
                                              hd[:, 0, ek, ttj * 128:(ttj + 1) * 128],
                                              qd[:, ek, n, :],
                                              start=(ek == 0), stop=False)
                    ctx.s_t.inc(mm)
                    ev[("tpmm_h", idx)] = ctx.s_t.n
                    t.wait_ge(ctx.ld_gp.h, ev[("hld", idx, 1)])
                    for ek in range(32, 64):
                        for ttj in range(2):
                            for n in range(2):
                                mm = t.matmul(ps[:, 4 * pp + 2 * ttj + n, :],
                                              hd[:, 1, ek - 32, ttj * 128:(ttj + 1) * 128],
                                              qd[:, ek, n, :],
                                              start=False, stop=(ek == 63))
                    ctx.s_t.inc(mm)
                ev[("tpmm", idx)] = ctx.s_t.n

                @b.scalar
                def _(sc, half=half, tp=tp, idx=idx, pp=pp):
                    sc.wait_ge(ctx.s_t.h, ev[("tpmm", idx)])
                    for u in range(4):
                        if idx >= 1:
                            sc.wait_ge(ctx.s_v.h, ev[("guse", idx - 1, u)])
                        ctx.s_s.inc(sc.activation(gb[:, u, :], ps[:, 4 * pp + u, :], AF.Silu))
                        ev[("g", idx, u)] = ctx.s_s.n

                @b.vector
                def _(v, half=half, tp=tp, idx=idx):
                    v.wait_ge(ctx.ld_sy.h, ev[("sld", idx)])
                    for u in range(4):
                        v.wait_ge(ctx.s_s.h, ev[("g", idx, u)])
                        if idx >= 1:
                            v.wait_ge(ctx.st_gp.h, ev[("ost", idx - 1, u)])
                        v.tensor_tensor(ob[:, u, :], gb[:, u, :], sxs[:, u, :], ALU.mult)
                        ctx.s_v.inc(v.tensor_tensor(ob[:, u, :], ob[:, u, :],
                                                    sxx[:, u, :], ALU.add))
                        ev[("guse", idx, u)] = ctx.s_v.n
                        ev[("oevu", idx, u)] = ctx.s_v.n
                    ev[("oev", idx)] = ctx.s_v.n

                @b.gpsimd
                def _(g, half=half, tp=tp, idx=idx):
                    for u in range(4):
                        ttj, n = u // 2, u % 2
                        tti = tp * 2 + ttj
                        c0 = half * 1024 + n * 512
                        g.wait_ge(ctx.s_v.h, ev[("oevu", idx, u)])
                        ctx.st_gp.dma(g.dma_start(
                            io.out[tti * 128:(tti + 1) * 128, c0:c0 + 512], ob[:, u, :]))
                        ev[("ost", idx, u)] = ctx.st_gp.n


# version 16
# speedup vs baseline: 1.0275x; 1.0275x over previous
from contextlib import ExitStack

import numpy as np
import ml_dtypes
import concourse.bass as bass
import concourse.mybir as mybir
from concourse.bass_utils import run_bass_kernel_spmd

F32 = mybir.dt.float32
BF16 = mybir.dt.bfloat16
AF = mybir.ActivationFunctionType
ALU = mybir.AluOpType
AXX = mybir.AxisListType.X

L = 4096
DM = 2048
INTER = 4096
H = 64
PD = 64
NST = 128
CH = 256
NCH = 16
CONV = 4352
OUTIN = 8512
DE = 8192
KC = 4

DEBUG = False
PHASES = "all"


def _bfarr(x):
    return np.ascontiguousarray(np.asarray(x).astype(ml_dtypes.bfloat16))


def host_constants():
    ident = np.eye(128, dtype=np.float32)
    ut = np.triu(np.ones((CH, CH), dtype=np.float32))
    ut_t = np.ascontiguousarray(ut.reshape(2, 128, CH).transpose(1, 0, 2))
    return {"c_ident": _bfarr(ident), "c_ut": ut_t}


class Sem:
    def __init__(self, h):
        self.h = h
        self.n = 0

    def inc(self, instr, k=1):
        instr.then_inc(self.h, k)
        self.n += k
        return instr

    def dma(self, instr):
        return self.inc(instr, 16)


class NS:
    pass


def build(debug=False):
    nc = bass.Bass("TRN2", target_bir_lowering=False)
    io = NS()
    io.x = nc.dram_tensor("x", [L, DM], F32, kind="ExternalInput").ap()
    io.W_in = nc.dram_tensor("W_in", [DM, OUTIN], F32, kind="ExternalInput").ap()
    io.conv_wT = nc.dram_tensor("conv_wT", [CONV, KC], F32, kind="ExternalInput").ap()
    io.conv_b = nc.dram_tensor("conv_b", [CONV], F32, kind="ExternalInput").ap()
    io.dt_bias = nc.dram_tensor("dt_bias", [H], F32, kind="ExternalInput").ap()
    io.A_log = nc.dram_tensor("A_log", [H], F32, kind="ExternalInput").ap()
    io.D = nc.dram_tensor("D", [H], F32, kind="ExternalInput").ap()
    io.norm_weight = nc.dram_tensor("norm_weight", [INTER], F32, kind="ExternalInput").ap()
    io.W_out = nc.dram_tensor("W_out", [INTER, DM], F32, kind="ExternalInput").ap()
    io.W_upT = nc.dram_tensor("W_upT", [DM, DE], F32, kind="ExternalInput").ap()
    io.W_gateT = nc.dram_tensor("W_gateT", [DM, DE], F32, kind="ExternalInput").ap()
    io.W_downT = nc.dram_tensor("W_downT", [DE, DM], F32, kind="ExternalInput").ap()
    io.c_ident = nc.dram_tensor("c_ident", [128, 128], BF16, kind="ExternalInput").ap()
    io.c_ut = nc.dram_tensor("c_ut", [128, 2, CH], F32, kind="ExternalInput").ap()
    io.out = nc.dram_tensor("out", [L, DM], F32, kind="ExternalOutput").ap()

    ARENA_WORDS = 57 * 1024 * 1024
    arenas = [nc.dram_tensor(f"scr_arena{i}", [ARENA_WORDS], F32, kind="Internal").ap()
              for i in range(2)]
    offs = [0, 0]

    def scratch(name, shape, dt, dbg=False, ar=0):
        if debug and dbg:
            return nc.dram_tensor(name, shape, dt, kind="ExternalOutput").ap()
        n = 1
        for s in shape:
            n *= s
        words = n if dt == F32 else (n + 1) // 2
        words = (words + 1023) // 1024 * 1024
        view = arenas[ar][offs[ar]:offs[ar] + words]
        offs[ar] += words
        assert offs[ar] <= ARENA_WORDS, name
        if dt == BF16:
            view = view.bitcast(BF16)
        assert len(shape) == 2
        return view[: shape[0] * shape[1]].rearrange("(a b) -> a b", b=shape[1])

    io.xn_tok = scratch("xn_tok", [L, DM], BF16, dbg=True)
    io.sz_tok = scratch("sz_tok", [L, INTER], F32, dbg=True)
    io.xbc_cm = scratch("xbc_cm", [CONV, L], F32, dbg=True)
    io.xc_cm = scratch("xc_cm", [INTER, L], BF16, dbg=True)
    io.yf_tok = scratch("yf_tok", [L, INTER], BF16, dbg=True)
    io.s_tok = scratch("s_tok", [L, DM], F32, dbg=True, ar=1)
    io.q_upT = scratch("q_upT", [DM, DE], BF16, dbg=True, ar=1)
    io.q_gateT = scratch("q_gateT", [DM, DE], BF16, ar=1)
    io.q_downT = scratch("q_downT", [DE, DM], BF16, ar=1)
    io.hid_cm = scratch("hid_cm", [DE, L], BF16, dbg=True, ar=1)
    io.srow = scratch("srow", [8, L], F32, ar=1)
    io.dbg = scratch("dbgout", [128, 16384], F32, dbg=True) if debug else None

    build_program(nc, io, debug)
    return nc


def kernel(**inputs):
    nc = build(DEBUG)
    consts = host_constants()
    shared = dict(
        W_in=np.ascontiguousarray(inputs["L0_attn_W_in"]),
        conv_wT=np.ascontiguousarray(np.asarray(inputs["L0_attn_conv_w"]).T),
        conv_b=np.ascontiguousarray(inputs["L0_attn_conv_b"]),
        dt_bias=np.ascontiguousarray(inputs["L0_attn_dt_bias"]),
        A_log=np.ascontiguousarray(inputs["L0_attn_A_log"]),
        D=np.ascontiguousarray(inputs["L0_attn_D"]),
        norm_weight=np.ascontiguousarray(inputs["L0_attn_norm_weight"]),
        W_out=np.ascontiguousarray(inputs["L0_attn_W_out"]),
        W_upT=np.ascontiguousarray(np.asarray(inputs["L0_ffn_W_up"]).T),
        W_gateT=np.ascontiguousarray(np.asarray(inputs["L0_ffn_W_gate"]).T),
        W_downT=np.ascontiguousarray(np.asarray(inputs["L0_ffn_W_down"]).T),
    )
    shared.update(consts)
    xs = np.asarray(inputs["L0_attn_hidden_states"])
    in_maps = [dict(shared, x=np.ascontiguousarray(xs[c])) for c in range(8)]
    res = run_bass_kernel_spmd(nc, in_maps, core_ids=list(range(8)))
    kernel.last_results = res.results
    return np.stack([np.asarray(res.results[c]["out"]) for c in range(8)], axis=0)


def build_program(nc, io, debug):
    with ExitStack() as stack:
        e = stack.enter_context
        h0 = e(nc.semaphore("ld_sy"))
        h1 = e(nc.semaphore("ld_gp"))
        h2 = e(nc.semaphore("ld_sc"))
        h3 = e(nc.semaphore("st_sy"))
        h4 = e(nc.semaphore("st_gp"))
        h5 = e(nc.semaphore("s_t"))
        h6 = e(nc.semaphore("s_v"))
        h7 = e(nc.semaphore("s_s"))
        h8 = e(nc.semaphore("s_g"))
        h9 = e(nc.semaphore("bar"))
        block = e(nc.Block())
        ctx = NS()
        ctx.nc, ctx.block = nc, block
        ctx.ld_sy, ctx.ld_gp, ctx.ld_sc = Sem(h0), Sem(h1), Sem(h2)
        ctx.st_sy, ctx.st_gp = Sem(h3), Sem(h4)
        ctx.s_t, ctx.s_v, ctx.s_s, ctx.s_g = Sem(h5), Sem(h6), Sem(h7), Sem(h8)
        ctx.bar_h = h9
        ctx.nbar = 0
        ctx.q_acc = e(nc.sbuf_tensor("q_acc", [128, 384], F32))
        ctx.q_scl = e(nc.sbuf_tensor("q_scl", [128, 16], F32))

        def barrier(exclude=()):
            ctx.nbar += 1
            tgt = 5 * ctx.nbar
            sems = [ctx.s_t, ctx.s_v, ctx.s_s, ctx.s_g,
                    ctx.ld_sy, ctx.ld_gp, ctx.ld_sc, ctx.st_sy, ctx.st_gp]
            vals = [(s.h, s.n) for s in sems if s.n > 0 and s not in exclude]
            for dec in (block.sync, block.tensor, block.vector, block.scalar, block.gpsimd):
                @dec
                def _(e):
                    for h, n in vals:
                        e.wait_ge(h, n)
                    e.sem_inc(ctx.bar_h, 1)
                    e.wait_ge(ctx.bar_h, tgt)
        ctx.barrier = barrier

        p2tiles = []
        for cc in range(8):
            for mj in (0, 1):
                for rt in range(16):
                    p2tiles.append((mj, rt, cc))
        for cc in range(2):
            for rt in range(64):
                p2tiles.append((2, rt, cc))
        ctx.p2i = [0]
        ctx.p2ev = {}
        qmats = [(io.W_upT, io.q_upT), (io.W_gateT, io.q_gateT), (io.W_downT, io.q_downT)]

        def emit_p2(count):
            b2, ev2 = block, ctx.p2ev
            scl = ctx.q_scl
            wt2, tmp2, qt2 = ctx.q_w2, ctx.q_tmp, ctx.q_qt
            for _ in range(count):
                i = ctx.p2i[0]
                if i >= len(p2tiles):
                    return
                mj, rt, cc = p2tiles[i]
                wr = qmats[mj][0].rearrange("(a p) c -> p a c", p=128)
                qr = qmats[mj][1].rearrange("(a p) c -> p a c", p=128)

                @b2.sync
                def _(s, i=i, rt=rt, cc=cc, wr=wr):
                    if i >= 2:
                        s.wait_ge(ctx.s_v.h, ev2[("q", i - 2)])
                    ctx.ld_sy.dma(s.dma_start(wt2[:, i % 2, :],
                                              wr[:, rt, cc * 1024:(cc + 1) * 1024]))
                ev2[("p2ld", i)] = ctx.ld_sy.n

                @b2.vector
                def _(v, i=i, mj=mj):
                    v.wait_ge(ctx.ld_sy.h, ev2[("p2ld", i)])
                    c0 = 3 * mj + 7
                    w = wt2[:, i % 2, :]
                    v.tensor_scalar(tmp2[:, :], w, scl[:, c0 + 1:c0 + 2], 1e30,
                                    ALU.add, ALU.mult)
                    v.tensor_scalar(tmp2[:, :], tmp2[:, :], scl[:, c0 + 2:c0 + 3], 0.0,
                                    ALU.max, ALU.min)
                    v.tensor_scalar(w, w, scl[:, c0 + 1:c0 + 2], 1e30,
                                    ALU.subtract, ALU.mult)
                    v.tensor_scalar(w, w, scl[:, c0:c0 + 1], 0.0,
                                    ALU.min, ALU.max)
                    if i >= 1:
                        v.wait_ge(ctx.st_gp.h, ev2[("qst", i - 1)])
                    ctx.s_v.inc(v.tensor_tensor(qt2[:, :], w, tmp2[:, :], ALU.add))
                ev2[("q", i)] = ctx.s_v.n

                @b2.gpsimd
                def _(g, i=i, rt=rt, cc=cc, qr=qr):
                    g.wait_ge(ctx.s_v.h, ev2[("q", i)])
                    ctx.st_gp.dma(g.dma_start(qr[:, rt, cc * 1024:(cc + 1) * 1024],
                                              qt2[:, :]))
                ev2[("qst", i)] = ctx.st_gp.n
                ctx.p2i[0] += 1
                if mj == 1 and rt == 15:
                    ev2[("qcc", cc)] = ctx.st_gp.n
                if ctx.p2i[0] == len(p2tiles):
                    ctx.ev_qdown = ctx.st_gp.n
        ctx.emit_p2 = emit_p2

        with ExitStack() as pstack:
            pe = pstack.enter_context
            sb = NS()
            sb.ident = pe(nc.sbuf_tensor("ident", [128, 128], BF16))
            sb.ut = pe(nc.sbuf_tensor("ut", [128, 2, CH], F32))
            sb.cw = pe(nc.sbuf_tensor("cw", [128, 34, KC], F32))
            sb.cb = pe(nc.sbuf_tensor("cb", [128, 34], F32))
            sb.dtb = pe(nc.sbuf_tensor("dtb_t", [128, H], F32))
            sb.a_t = pe(nc.sbuf_tensor("a_t", [128, H], F32))
            sb.d_t = pe(nc.sbuf_tensor("d_t", [128, H], F32))
            sb.dtraw = pe(nc.sbuf_tensor("dt_raw", [128, 32 * H], F32))
            sb.tiny = pe(nc.sbuf_tensor("tiny", [128, 64], F32))
            sb.wt1 = pe(nc.sbuf_tensor("q_w1", [128, 2, 1024], F32))

            p0_consts(ctx, io, sb)
            ctx.barrier()
            p1_ln(ctx, io, sb)
            ctx.barrier()
            if PHASES != "p1":
                with nc.sbuf_tensor("xnT", [128, 16, L], BF16) as sb_xnT:
                    sb.xnT = sb_xnT
                    p2_win(ctx, io, sb)
                    ctx.barrier(exclude=(ctx.ld_gp, ctx.s_g))
                if PHASES != "p2":
                    with ExitStack() as lstack:
                        le = lstack.enter_context
                        sb.nw = le(nc.sbuf_tensor("normw", [128, INTER], F32))
                        sb.expA = le(nc.sbuf_tensor("expA", [128, 32 * H], F32))
                        sb.expB = le(nc.sbuf_tensor("expB", [128, 32 * H], F32))
                        sb.erows = le(nc.sbuf_tensor("erows", [128, NCH * 192], F32))
                        sb.bt = le(nc.sbuf_tensor("btr", [128, L], BF16))
                        sb.ct = le(nc.sbuf_tensor("ctr", [128, L], BF16))
                        p3_conv(ctx, io, sb)
                        ctx.barrier(exclude=(ctx.ld_gp, ctx.s_g))
                        if PHASES != "p3":
                            p4_dt(ctx, io, sb)
                            ctx.barrier(exclude=(ctx.ld_gp, ctx.s_g))
                            if PHASES not in ("p4",):
                                p5_ssd(ctx, io, sb)
                                ctx.barrier(exclude=(ctx.ld_gp, ctx.s_g))
        if PHASES not in ("p1", "p2", "p3", "p4", "p5"):
            with ExitStack() as qstack:
                qe = qstack.enter_context
                ctx.q_w2 = qe(nc.sbuf_tensor("q_w2", [128, 2, 1024], F32))
                ctx.q_tmp = qe(nc.sbuf_tensor("q_tmp", [128, 1024], F32))
                ctx.q_qt = qe(nc.sbuf_tensor("q_qt", [128, 1024], BF16))
                ctx.q_idb = ctx.q_qt
                ctx.q_id32 = ctx.q_tmp
                p7_wout(ctx, io, sb)
                ctx.barrier()
                if PHASES in ("all", "p8", "p9"):
                    p9_upgate(ctx, io)
                    ctx.barrier()
                    if PHASES == "all":
                        p10_down(ctx, io)
        finish(ctx, io)


def finish(ctx, io):
    @ctx.block.gpsimd
    def _(g):
        g.wait_ge(ctx.st_gp.h, ctx.st_gp.n)

    @ctx.block.sync
    def _(s):
        s.wait_ge(ctx.st_gp.h, ctx.st_gp.n)
        s.wait_ge(ctx.st_sy.h, ctx.st_sy.n)


def p0_consts(ctx, io, sb):
    b = ctx.block
    ld, st = ctx.ld_sy, ctx.st_sy

    @b.sync
    def _(s):
        ld.dma(s.dma_start(sb.ident[:, :], io.c_ident[:, :]))
        ld.dma(s.dma_start(sb.ut[:, :, :], io.c_ut[:, :, :]))
        ld.dma(s.dma_start(sb.tiny[0:1, 0:H], io.A_log.rearrange("(o m) -> o m", o=1)))
        ld.dma(s.dma_start(sb.dtb[:, :],
                           io.dt_bias.rearrange("(o m) -> o m", o=1).partition_broadcast(128)))
        ld.dma(s.dma_start(sb.d_t[:, :],
                           io.D.rearrange("(o m) -> o m", o=1).partition_broadcast(128)))
        for t in range(34):
            ld.dma(s.dma_start(sb.cw[:, t, :], io.conv_wT[t * 128:(t + 1) * 128, :]))
            ld.dma(s.dma_start(
                sb.cb[:, t:t + 1],
                io.conv_b[t * 128:(t + 1) * 128].rearrange("(p o) -> p o", o=1)))
    ld_all = ld.n

    @b.scalar
    def _(sc):
        sc.wait_ge(ld.h, ld_all)
        ctx.s_s.inc(sc.activation(sb.tiny[0:1, 0:H], sb.tiny[0:1, 0:H], AF.Exp))
    ev_exp = ctx.s_s.n

    @b.vector
    def _(v):
        v.wait_ge(ctx.s_s.h, ev_exp)
        ctx.s_v.inc(v.tensor_scalar(sb.tiny[0:1, 0:H], sb.tiny[0:1, 0:H], -1.0, None, ALU.mult))
    ev_neg = ctx.s_v.n

    @b.sync
    def _(s):
        s.wait_ge(ctx.s_v.h, ev_neg)
        st.dma(s.dma_start(io.srow[0:1, 0:H], sb.tiny[0:1, 0:H]))
        s.wait_ge(st.h, st.n)
        ld.dma(s.dma_start(sb.a_t[:, :], io.srow[0:1, 0:H].partition_broadcast(128)))


def p1_ln(ctx, io, sb):
    nc, b = ctx.nc, ctx.block
    NT = 32
    with (
        nc.sbuf_tensor("p1_x", [128, 2, DM], F32) as xin,
        nc.sbuf_tensor("p1_sq", [128, 2, DM], F32) as sq,
        nc.sbuf_tensor("p1_xn", [128, 2, DM], BF16) as xnb,
        nc.sbuf_tensor("p1_cols", [128, 2, 8], F32) as cols,
    ):
        ld0, v0, s0, st0 = ctx.ld_sy.n, ctx.s_v.n, ctx.s_s.n, ctx.st_sy.n

        @b.sync
        def _(s):
            for i in range(NT):
                pp = i % 2
                if i >= 2:
                    s.wait_ge(ctx.s_s.h, s0 + 4 * (i - 1))
                ctx.ld_sy.dma(s.dma_start(xin[:, pp, :], io.x[i * 128:(i + 1) * 128, :]))
                if i >= 1:
                    s.wait_ge(ctx.s_s.h, s0 + 4 * i)
                    ctx.st_sy.dma(s.dma_start(io.xn_tok[(i - 1) * 128:i * 128, :],
                                              xnb[:, (i - 1) % 2, :]))
            s.wait_ge(ctx.s_s.h, s0 + 4 * NT)
            ctx.st_sy.dma(s.dma_start(io.xn_tok[(NT - 1) * 128:NT * 128, :],
                                      xnb[:, (NT - 1) % 2, :]))

        @b.vector
        def _(v):
            for i in range(NT):
                pp = i % 2
                v.wait_ge(ctx.ld_sy.h, ld0 + 16 * (i + 1))
                ctx.s_v.inc(v.tensor_reduce(cols[:, pp, 0:1], xin[:, pp, :], AXX, ALU.add))
                v.wait_ge(ctx.s_v.h, ctx.s_v.n)
                ctx.s_v.inc(v.tensor_scalar(cols[:, pp, 1:2], cols[:, pp, 0:1],
                                            -1.0 / DM, None, ALU.mult))
                v.wait_ge(ctx.s_s.h, s0 + 4 * i + 1)
                ctx.s_v.inc(v.tensor_reduce(cols[:, pp, 2:3], sq[:, pp, :], AXX, ALU.add))
                v.wait_ge(ctx.s_v.h, ctx.s_v.n)
                ctx.s_v.inc(v.tensor_scalar(cols[:, pp, 3:4], cols[:, pp, 2:3],
                                            1.0 / DM, 1e-5, ALU.mult, ALU.add))
                v.wait_ge(ctx.s_s.h, s0 + 4 * i + 3)
                ctx.s_v.inc(v.tensor_tensor(cols[:, pp, 6:7], cols[:, pp, 1:2],
                                            cols[:, pp, 5:6], ALU.mult))

        @b.scalar
        def _(sc):
            for i in range(NT):
                pp = i % 2
                sc.wait_ge(ctx.s_v.h, v0 + 5 * i + 2)
                ctx.s_s.inc(sc.activation(sq[:, pp, :], xin[:, pp, :], AF.Square,
                                          bias=cols[:, pp, 1:2], scale=1.0))
                sc.wait_ge(ctx.s_v.h, v0 + 5 * i + 4)
                ctx.s_s.inc(sc.activation(cols[:, pp, 4:5], cols[:, pp, 3:4], AF.Ln))
                sc.wait_ge(ctx.s_s.h, ctx.s_s.n)
                ctx.s_s.inc(sc.activation(cols[:, pp, 5:6], cols[:, pp, 4:5], AF.Exp, scale=-0.5))
                sc.wait_ge(ctx.s_v.h, v0 + 5 * i + 5)
                sc.wait_ge(ctx.s_s.h, ctx.s_s.n)
                if i >= 2:
                    sc.wait_ge(ctx.st_sy.h, st0 + 16 * (i - 1))
                ctx.s_s.inc(sc.activation(xnb[:, pp, :], xin[:, pp, :], AF.Identity,
                                          bias=cols[:, pp, 6:7], scale=cols[:, pp, 5:6]))


def p2_win(ctx, io, sb):
    nc, b = ctx.nc, ctx.block
    ld_sc0, ld_gp0 = ctx.ld_sc.n, ctx.ld_gp.n
    t0, v0, s0 = ctx.s_t.n, ctx.s_v.n, ctx.s_s.n
    st_sy0, st_gp0 = ctx.st_sy.n, ctx.st_gp.n

    @b.scalar
    def _(sc):
        for sl in range(8):
            for k in range(16):
                ctx.ld_sc.dma(sc.dma_start_transpose(
                    sb.xnT[:, k, sl * 512:(sl + 1) * 512],
                    io.xn_tok[sl * 512:(sl + 1) * 512, k * 128:(k + 1) * 128]))

    with (
        nc.sbuf_tensor("p2_wdt", [128, 16, H], BF16) as wdt,
        nc.sbuf_tensor("p2_wz", [128, 2, 16, 512], BF16) as wz,
        nc.sbuf_tensor("p2_wx", [128, 2, 16, 128], BF16) as wx,
        nc.sbuf_tensor("p2_zst", [128, 2, 512], F32) as zst,
        nc.sbuf_tensor("p2_xst", [128, 2, 512], F32) as xst,
        nc.psum_tensor("p2_psdt", [128, 2, 512], F32) as psdt,
        nc.psum_tensor("p2_psz", [128, 2, 512], F32) as psz,
        nc.psum_tensor("p2_psx", [128, 2, 512], F32) as psx,
    ):
        wiT = io.W_in.rearrange("(a p) c -> p a c", p=128)
        ev = {}

        @b.gpsimd
        def _(g):
            ctx.ld_gp.dma(g.dma_start(wdt[:, :, :], wiT[:, :, 8448:8512]))
            for nb in range(8):
                if nb >= 2:
                    g.wait_ge(ctx.s_t.h, t0 + 32 + 32 * (nb - 1))
                ctx.ld_gp.dma(g.dma_start(wz[:, nb % 2, :, :], wiT[:, :, nb * 512:(nb + 1) * 512]))

            def store_block(mb2):
                for nb in range(8):
                    idx = mb2 * 8 + nb
                    g.wait_ge(ctx.s_v.h, v0 + idx + 1)
                    ctx.st_gp.dma(g.dma_start(
                        io.xbc_cm[mb2 * 128:(mb2 + 1) * 128, nb * 512:(nb + 1) * 512],
                        xst[:, idx % 2, :]))

            for mb in range(34):
                if mb >= 2:
                    store_block(mb - 2)
                    g.wait_ge(ctx.s_t.h, t0 + 32 + 256 + 8 * (mb - 1))
                ctx.ld_gp.dma(g.dma_start(wx[:, mb % 2, :, :],
                                          wiT[:, :, 4096 + mb * 128:4096 + (mb + 1) * 128]))
            store_block(32)
            store_block(33)

        @b.tensor
        def _(t):
            t.wait_ge(ctx.ld_gp.h, ld_gp0 + 16)
            for m in range(32):
                t.wait_ge(ctx.ld_sc.h, ld_sc0 + 256 * (m // 4 + 1))
                if m >= 2:
                    t.wait_ge(ctx.s_s.h, s0 + (m - 1))
                mm = None
                for k in range(16):
                    mm = t.matmul(psdt[:, m % 2, 0:H], sb.xnT[:, k, m * 128:(m + 1) * 128],
                                  wdt[:, k, :], start=(k == 0), stop=(k == 15))
                ctx.s_t.inc(mm)
            for nb in range(8):
                t.wait_ge(ctx.ld_gp.h, ld_gp0 + 16 + 16 * (nb + 1))
                for m in range(32):
                    idx = nb * 32 + m
                    t.wait_ge(ctx.ld_sc.h, ld_sc0 + 256 * (m // 4 + 1))
                    if idx >= 2:
                        t.wait_ge(ctx.s_s.h, s0 + 32 + (idx - 1))
                    mm = None
                    for k in range(16):
                        mm = t.matmul(psz[:, idx % 2, :], sb.xnT[:, k, m * 128:(m + 1) * 128],
                                      wz[:, nb % 2, k, :], start=(k == 0), stop=(k == 15))
                    ctx.s_t.inc(mm)
            for mb in range(34):
                t.wait_ge(ctx.ld_gp.h, ld_gp0 + 16 + 128 + 16 * (mb + 1))
                for nb in range(8):
                    idx = mb * 8 + nb
                    if idx >= 2:
                        t.wait_ge(ctx.s_v.h, v0 + (idx - 1))
                    mm = None
                    for k in range(16):
                        mm = t.matmul(psx[:, idx % 2, :], wx[:, mb % 2, k, :],
                                      sb.xnT[:, k, nb * 512:(nb + 1) * 512],
                                      start=(k == 0), stop=(k == 15))
                    ctx.s_t.inc(mm)

        @b.scalar
        def _(sc):
            for m in range(32):
                sc.wait_ge(ctx.s_t.h, t0 + m + 1)
                ctx.s_s.inc(sc.copy(sb.dtraw[:, m * H:(m + 1) * H], psdt[:, m % 2, 0:H]))
            for nb in range(8):
                for m in range(32):
                    idx = nb * 32 + m
                    sc.wait_ge(ctx.s_t.h, t0 + 32 + idx + 1)
                    if idx >= 2:
                        sc.wait_ge(ctx.st_sy.h, st_sy0 + 16 * (idx - 1))
                    ctx.s_s.inc(sc.activation(zst[:, idx % 2, :], psz[:, idx % 2, :], AF.Silu))

        @b.sync
        def _(s):
            for nb in range(8):
                for m in range(32):
                    idx = nb * 32 + m
                    s.wait_ge(ctx.s_s.h, s0 + 32 + idx + 1)
                    ctx.st_sy.dma(s.dma_start(
                        io.sz_tok[m * 128:(m + 1) * 128, nb * 512:(nb + 1) * 512],
                        zst[:, idx % 2, :]))

        @b.vector
        def _(v):
            for mb in range(34):
                for nb in range(8):
                    idx = mb * 8 + nb
                    v.wait_ge(ctx.s_t.h, t0 + 32 + 256 + idx + 1)
                    if idx >= 2:
                        v.wait_ge(ctx.st_gp.h, st_gp0 + 16 * (idx - 1))
                    ctx.s_v.inc(v.tensor_copy(xst[:, idx % 2, :], psx[:, idx % 2, :]))

        sg0 = ctx.s_g.n

        p1tiles = []
        for wsrc in (io.W_upT, io.W_gateT, io.W_downT):
            wr1 = wsrc.rearrange("(a p) c -> p a c", p=128)
            for rt in range(wsrc.shape[0] // 128):
                for cc in range(wsrc.shape[1] // 1024):
                    p1tiles.append((wr1, rt, cc))
        assert len(p1tiles) == 384
        ld_gp_p1base = ctx.ld_gp.n
        ctx.p1i = [0]

        def emit_p1_load(i):
            wr1, rt, cc = p1tiles[i]

            @b.gpsimd
            def _(g, i=i, wr1=wr1, rt=rt, cc=cc):
                if i >= 2:
                    g.wait_ge(ctx.s_g.h, sg0 + i - 1)
                ctx.ld_gp.dma(g.dma_start(sb.wt1[:, i % 2, :],
                                          wr1[:, rt, cc * 1024:(cc + 1) * 1024]))
        emit_p1_load(0)

        def emit_p1(count):
            for _ in range(count):
                i = ctx.p1i[0]
                if i >= 384:
                    return
                if i + 1 < 384:
                    emit_p1_load(i + 1)

                @b.scalar
                def _(sc, i=i):
                    sc.wait_ge(ctx.ld_gp.h, ld_gp_p1base + 16 * (i + 1))
                    ctx.s_g.inc(sc.activation(sb.wt1[:, i % 2, :], sb.wt1[:, i % 2, :],
                                              AF.Abs, accum_out=ctx.q_acc[:, i:i + 1]))
                ctx.p1i[0] += 1
        ctx.emit_p1 = emit_p1
        ctx.ev_p1done = sg0 + 384
        emit_p1(96)

def p3_conv(ctx, io, sb):
    nc, b = ctx.nc, ctx.block
    ld0, s0, st0 = ctx.ld_sy.n, ctx.s_s.n, ctx.st_sy.n
    with (
        nc.sbuf_tensor("p3_pb", [128, 2, L + 3], F32) as pb,
        nc.sbuf_tensor("p3_oa", [128, 2, L], F32) as oa,
        nc.sbuf_tensor("p3_ob", [128, 2, L], F32) as ob,
        nc.sbuf_tensor("p3_xc", [128, 2, L], BF16) as xcs,
    ):
        @b.vector
        def _(v):
            v.memset(pb[:, 0, 0:3], 0.0)
            ctx.s_v.inc(v.memset(pb[:, 1, 0:3], 0.0))
        v0 = ctx.s_v.n

        @b.sync
        def _(s):
            ctx.ld_sy.dma(s.dma_start(
                sb.nw[:, :],
                io.norm_weight.rearrange("(o m) -> o m", o=1).partition_broadcast(128)))
            s.wait_ge(ctx.s_v.h, v0)
            for i in range(34):
                pp = i % 2
                if i >= 2:
                    s.wait_ge(ctx.s_v.h, v0 + (i - 1))
                ctx.ld_sy.dma(s.dma_start(pb[:, pp, 3:L + 3],
                                          io.xbc_cm[i * 128:(i + 1) * 128, :]))
                if i >= 2 and i - 2 < 32:
                    s.wait_ge(ctx.s_s.h, s0 + (i - 1))
                    ctx.st_sy.dma(s.dma_start(io.xc_cm[(i - 2) * 128:(i - 1) * 128, :],
                                              xcs[:, i % 2, :]))
            for i in (32, 33):
                if i - 2 < 32:
                    s.wait_ge(ctx.s_s.h, s0 + (i - 1))
                    ctx.st_sy.dma(s.dma_start(io.xc_cm[(i - 2) * 128:(i - 1) * 128, :],
                                              xcs[:, i % 2, :]))

        @b.vector
        def _(v):
            for i in range(34):
                pp = i % 2
                v.wait_ge(ctx.ld_sy.h, ld0 + 16 * (i + 2))
                if i >= 2:
                    v.wait_ge(ctx.s_s.h, s0 + (i - 1))
                v.tensor_scalar(oa[:, pp, :], pb[:, pp, 0:L],
                                sb.cw[:, i, 0:1], sb.cb[:, i:i + 1], ALU.mult, ALU.add)
                v.scalar_tensor_tensor(ob[:, pp, :], pb[:, pp, 1:L + 1],
                                       sb.cw[:, i, 1:2], oa[:, pp, :], ALU.mult, ALU.add)
                v.scalar_tensor_tensor(oa[:, pp, :], pb[:, pp, 2:L + 2],
                                       sb.cw[:, i, 2:3], ob[:, pp, :], ALU.mult, ALU.add)
                ctx.s_v.inc(v.scalar_tensor_tensor(ob[:, pp, :], pb[:, pp, 3:L + 3],
                                                   sb.cw[:, i, 3:4], oa[:, pp, :],
                                                   ALU.mult, ALU.add))

        for i in range(34):
            @b.scalar
            def _(sc, i=i):
                pp = i % 2
                sc.wait_ge(ctx.s_v.h, v0 + (i + 1))
                if i < 32:
                    if i >= 2:
                        sc.wait_ge(ctx.st_sy.h, st0 + 16 * (i - 1))
                    ctx.s_s.inc(sc.activation(xcs[:, pp, :], ob[:, pp, :], AF.Silu))
                elif i == 32:
                    ctx.s_s.inc(sc.activation(sb.bt[:, :], ob[:, pp, :], AF.Silu))
                else:
                    ctx.s_s.inc(sc.activation(sb.ct[:, :], ob[:, pp, :], AF.Silu))
            ctx.emit_p1(4)

def p4_dt(ctx, io, sb):
    nc, b = ctx.nc, ctx.block
    with (
        nc.sbuf_tensor("p4_du", [128, 32 * H], F32) as du,
        nc.sbuf_tensor("p4_t1", [128, 32 * H], F32) as t1,
        nc.sbuf_tensor("p4_t2", [128, 32 * H], F32) as t2,
        nc.sbuf_tensor("p4_dtc", [128, 32 * H], F32) as dtc,
        nc.sbuf_tensor("p4_ldt", [128, 32 * H], F32) as ldt,
        nc.sbuf_tensor("p4_da", [128, 32 * H], F32) as dA,
        nc.sbuf_tensor("p4_acs", [128, 32 * H], F32) as acs,
        nc.sbuf_tensor("p4_ab", [128, 32 * H], F32) as ab,
        nc.sbuf_tensor("p4_rows", [1, NCH * 192], F32) as rows,
        nc.sbuf_tensor("p4_rr", [128, NCH * 192], F32) as rrows,
        nc.psum_tensor("p4_ps", [128, 2, 512], F32) as ps,
    ):
        @b.vector
        def _(v):
            v.tensor_tensor(
                du.rearrange("p (t h) -> p t h", h=H)[:, :, :],
                sb.dtraw.rearrange("p (t h) -> p t h", h=H)[:, :, :],
                sb.dtb.rearrange("p (o h) -> p o h", o=1).broadcast_to([128, 32, H]), ALU.add)
            ctx.s_v.inc(v.tensor_scalar(t2[:, :], du[:, :], 0.0, None, ALU.max))
        ev_v1 = ctx.s_v.n

        @b.scalar
        def _(sc):
            sc.wait_ge(ctx.s_v.h, ev_v1)
            sc.activation(t1[:, :], du[:, :], AF.Abs)
            sc.activation(t1[:, :], t1[:, :], AF.Exp, scale=-1.0)
            ctx.s_s.inc(sc.activation(t1[:, :], t1[:, :], AF.Ln, bias=1.0))
        ev_s1 = ctx.s_s.n

        @b.vector
        def _(v):
            v.wait_ge(ctx.s_s.h, ev_s1)
            v.tensor_tensor(dtc[:, :], t2[:, :], t1[:, :], ALU.add)
            ctx.s_v.inc(v.tensor_scalar(dtc[:, :], dtc[:, :], 100.0, None, ALU.min))
        ev_v2 = ctx.s_v.n

        @b.scalar
        def _(sc):
            sc.wait_ge(ctx.s_v.h, ev_v2)
            ctx.s_s.inc(sc.activation(ldt[:, :], dtc[:, :], AF.Ln))
        ev_s2 = ctx.s_s.n

        @b.vector
        def _(v):
            ctx.s_v.inc(v.tensor_tensor(
                dA.rearrange("p (t h) -> p t h", h=H)[:, :, :],
                dtc.rearrange("p (t h) -> p t h", h=H)[:, :, :],
                sb.a_t.rearrange("p (o h) -> p o h", o=1).broadcast_to([128, 32, H]), ALU.mult))
        ev_v3 = ctx.s_v.n

        t_0 = ctx.s_t.n

        @b.tensor
        def _(t):
            t.wait_ge(ctx.s_v.h, ev_v3)
            for c in range(NCH):
                for mt in range(2):
                    idx = 2 * c + mt
                    if idx >= 2:
                        t.wait_ge(ctx.s_s.h, ev_s2 + (idx - 1))
                    mm = None
                    for kt in range(2):
                        mm = t.matmul(ps[:, idx % 2, 0:H],
                                      sb.ut[:, kt, mt * 128:(mt + 1) * 128],
                                      dA[:, (2 * c + kt) * H:(2 * c + kt + 1) * H],
                                      start=(kt == 0), stop=(kt == 1))
                    ctx.s_t.inc(mm)

        @b.scalar
        def _(sc):
            for idx in range(32):
                sc.wait_ge(ctx.s_t.h, t_0 + idx + 1)
                ctx.s_s.inc(sc.copy(acs[:, idx * H:(idx + 1) * H], ps[:, idx % 2, 0:H]))
        ev_s3 = ctx.s_s.n

        @b.sync
        def _(s):
            s.wait_ge(ctx.s_s.h, ev_s3)
            for c in range(NCH):
                ctx.st_sy.dma(s.dma_start(rows[0:1, c * 192:c * 192 + H],
                                          acs[0:1, (2 * c) * H:(2 * c + 1) * H]))
                ctx.st_sy.dma(s.dma_start(rows[0:1, c * 192 + H:c * 192 + 2 * H],
                                          acs[127:128, (2 * c + 1) * H:(2 * c + 2) * H]))
        ev_st1 = ctx.st_sy.n

        @b.vector
        def _(v):
            v.wait_ge(ctx.st_sy.h, ev_st1)
            r4 = rows.rearrange("o (c k h) -> o c k h", k=3, h=H)
            ctx.s_v.inc(v.tensor_tensor(r4[:, :, 2, :], r4[:, :, 1, :], r4[:, :, 0, :],
                                        ALU.subtract))
        ev_v4 = ctx.s_v.n

        @b.sync
        def _(s):
            s.wait_ge(ctx.s_v.h, ev_v4)
            ctx.st_sy.dma(s.dma_start(io.srow[1:2, 0:NCH * 192], rows[0:1, :]))
            s.wait_ge(ctx.st_sy.h, ctx.st_sy.n)
            ctx.ld_sy.dma(s.dma_start(rrows[:, :], io.srow[1:2, 0:NCH * 192].partition_broadcast(128)))
        ev_ld1 = ctx.ld_sy.n

        @b.vector
        def _(v):
            v.wait_ge(ctx.ld_sy.h, ev_ld1)
            mm = None
            for c in range(NCH):
                mm = v.tensor_tensor(
                    ab.rearrange("p (t h) -> p t h", h=H)[:, 2 * c:2 * c + 2, :],
                    acs.rearrange("p (t h) -> p t h", h=H)[:, 2 * c:2 * c + 2, :],
                    rrows[:, c * 192:c * 192 + H].rearrange("p (o h) -> p o h", o=1)
                    .broadcast_to([128, 2, H]), ALU.subtract)
            ctx.s_v.inc(mm)
            v.wait_ge(ctx.s_v.h, ctx.s_v.n)
            v.tensor_tensor(ldt[:, :], ldt[:, :], ab[:, :], ALU.subtract)
            ctx.s_v.inc(v.tensor_scalar(ldt[:, :], ldt[:, :], 80.0, None, ALU.min))
        ev_v5 = ctx.s_v.n

        @b.scalar
        def _(sc):
            sc.wait_ge(ctx.s_v.h, ev_v5)
            sc.activation(sb.expA[:, :], ab[:, :], AF.Exp)
            sc.activation(sb.expB[:, :], ldt[:, :], AF.Exp)
            ctx.s_s.inc(sc.activation(sb.erows[:, :], rrows[:, :], AF.Exp))
        ev_s4 = ctx.s_s.n

        if io.dbg is not None:
            @b.sync
            def _(s):
                s.wait_ge(ctx.s_s.h, ev_s4)
                ctx.st_sy.dma(s.dma_start(io.dbg[:, 0:2048], dtc[:, :]))
                ctx.st_sy.dma(s.dma_start(io.dbg[:, 2048:4096], acs[:, :]))
                ctx.st_sy.dma(s.dma_start(io.dbg[:, 4096:6144], sb.expA[:, :]))
                ctx.st_sy.dma(s.dma_start(io.dbg[:, 6144:8192], sb.expB[:, :]))
                ctx.st_sy.dma(s.dma_start(io.dbg[:, 8192:8192 + 3072], sb.erows[:, :]))

def p5_ssd(ctx, io, sb):
    nc, b = ctx.nc, ctx.block
    with (
        nc.sbuf_tensor("p5_x2", [128, 2, INTER], BF16) as x2,
        nc.sbuf_tensor("p5_xb", [128, 2, INTER], BF16) as xb,
        nc.sbuf_tensor("p5_cbt", [128, 2, 2, CH], BF16) as cbt,
        nc.sbuf_tensor("p5_btok", [128, 2, 2, 128], BF16) as btok,
        nc.sbuf_tensor("p5_sA", [128, INTER], F32) as sA,
        nc.sbuf_tensor("p5_sB", [128, INTER], F32) as sB,
        nc.sbuf_tensor("p5_ssc", [128, INTER], BF16) as ssc,
        nc.sbuf_tensor("p5_yb", [128, INTER], BF16) as yb,
        nc.sbuf_tensor("p5_sz", [128, INTER], F32) as szb,
        nc.sbuf_tensor("p5_ts", [128, INTER], F32) as tsc,
        nc.sbuf_tensor("p5_yfb", [128, INTER], BF16) as yfb,
        nc.sbuf_tensor("p5_cols", [128, 16], F32) as cols,
        nc.psum_tensor("p5_pst", [128, 4, 512], F32) as pst,
        nc.psum_tensor("p5_psy", [128, 2, 512], F32) as psy,
        nc.psum_tensor("p5_psc", [128, 512], F32) as psc,
        nc.psum_tensor("p5_psb", [128, 1024], BF16) as psb,
    ):
        ev = {}

        @b.vector
        def _(v):
            v.memset(sA[:, :], 0.0)
            ctx.s_v.inc(v.memset(cols[:, 8:9], 1e-8))
        ev[("sinit",)] = ctx.s_v.n
        yidx = [0]

        def er_slice(c, kind, h0, nh):
            off = c * 192 + {"r": 0, "ce": H, "cer": 2 * H}[kind] + h0
            return sb.erows[:, off:off + nh]

        for c in range(NCH):
            S_in = sA if c % 2 == 0 else sB
            S_out = sB if c % 2 == 0 else sA
            pp = c % 2

            @b.scalar
            def _(sc, c=c):
                if c >= 1:
                    sc.wait_ge(ctx.s_v.h, ev[("dterm", c - 1, 1)])
                for lt in range(2):
                    ctx.ld_sc.dma(sc.dma_start_transpose(
                        x2[:, lt, :], io.xc_cm[:, (2 * c + lt) * 128:(2 * c + lt + 1) * 128]))
            ev[("x2", c)] = ctx.ld_sc.n

            @b.tensor
            def _(t, c=c, pp=pp):
                if c >= 1:
                    t.wait_ge(ctx.s_v.h, ev[("cbtev", c - 1)])
                mm = None
                for st in range(2):
                    mm = t.matmul(psc[:, st * 256:(st + 1) * 256],
                                  sb.bt[:, c * 256 + st * 128:c * 256 + (st + 1) * 128],
                                  sb.ct[:, c * 256:(c + 1) * 256], start=True, stop=True)
                ctx.s_t.inc(mm)
                if c >= 1:
                    t.wait_ge(ctx.s_v.h, ev[("btokev", c - 1)])
                mm = None
                for st in range(2):
                    mm = t.transpose(psb[:, st * 128:(st + 1) * 128],
                                     sb.bt[:, c * 256 + st * 128:c * 256 + (st + 1) * 128],
                                     sb.ident[:, :])
                ctx.s_t.inc(mm)
            ev[("cbtmm", c)] = ctx.s_t.n - 1
            ev[("btokmm", c)] = ctx.s_t.n

            @b.vector
            def _(v, c=c, pp=pp, S_in=S_in):
                v.wait_ge(ctx.s_t.h, ev[("cbtmm", c)])
                mm = None
                for st in range(2):
                    mm = v.tensor_tensor(cbt[:, pp, st, :], psc[:, st * 256:(st + 1) * 256],
                                         sb.ut[:, st, :], ALU.mult)
                ctx.s_v.inc(mm)
                ev[("cbtev", c)] = ctx.s_v.n
                v.wait_ge(ctx.s_t.h, ev[("btokmm", c)])
                for st in range(2):
                    mm = v.tensor_copy(btok[:, pp, st, :], psb[:, st * 128:(st + 1) * 128])
                ctx.s_v.inc(mm)
                ev[("btokev", c)] = ctx.s_v.n
                for half in range(2):
                    mm = v.tensor_tensor(
                        ssc.rearrange("p (h q) -> p h q", q=PD)[:, half * 32:(half + 1) * 32, :],
                        S_in.rearrange("p (h q) -> p h q", q=PD)[:, half * 32:(half + 1) * 32, :],
                        er_slice(c, "r", half * 32, 32).rearrange("p (h o) -> p h o", o=1)
                        .broadcast_to([128, 32, PD]), ALU.mult)
                ctx.s_v.inc(mm)
                ev[("ssc", c)] = ctx.s_v.n
                v.wait_ge(ctx.ld_sc.h, ev[("x2", c)])
                for lt in range(2):
                    mm = v.tensor_tensor(
                        xb.rearrange("p s (h q) -> p s h q", q=PD)[:, lt, :, :],
                        x2.rearrange("p s (h q) -> p s h q", q=PD)[:, lt, :, :],
                        sb.expB[:, (2 * c + lt) * H:(2 * c + lt + 1) * H]
                        .rearrange("p (h o) -> p h o", o=1).broadcast_to([128, H, PD]),
                        ALU.mult)
                ctx.s_v.inc(mm)
                ev[("xb", c)] = ctx.s_v.n

            for half in range(2):
                @b.tensor
                def _(t, c=c, pp=pp, half=half):
                    if half == 0:
                        t.wait_ge(ctx.s_v.h, ev[("xb", c)])
                        if c >= 1:
                            t.wait_ge(ctx.s_v.h, ev[("supd", c - 1, 1)])
                    else:
                        t.wait_ge(ctx.s_v.h, ev[("supd", c, 0)])
                    mm = None
                    for j in range(4):
                        for st in range(2):
                            mm = t.matmul(pst[:, j, :], btok[:, pp, st, :],
                                          xb[:, st, half * 2048 + j * 512:half * 2048 + (j + 1) * 512],
                                          start=(st == 0), stop=(st == 1))
                    ctx.s_t.inc(mm)
                    ev[("states", c, half)] = ctx.s_t.n

                @b.vector
                def _(v, c=c, S_in=S_in, S_out=S_out, half=half):
                    if c >= 1 and half == 0:
                        v.wait_ge(ctx.s_s.h, ev[("yn", c - 1, 1)])
                    v.wait_ge(ctx.s_t.h, ev[("states", c, half)])
                    sl = slice(half * 2048, (half + 1) * 2048)
                    h0 = half * 32
                    pst_f = pst.rearrange("p j n -> p (j n)")
                    v.tensor_tensor(
                        tsc.rearrange("p (h q) -> p h q", q=PD)[:, h0:h0 + 32, :],
                        pst_f.rearrange("p (h q) -> p h q", q=PD)[:, 0:32, :],
                        er_slice(c, "cer", h0, 32).rearrange("p (h o) -> p h o", o=1)
                        .broadcast_to([128, 32, PD]), ALU.mult)
                    v.tensor_tensor(
                        S_out.rearrange("p (h q) -> p h q", q=PD)[:, h0:h0 + 32, :],
                        S_in.rearrange("p (h q) -> p h q", q=PD)[:, h0:h0 + 32, :],
                        er_slice(c, "ce", h0, 32).rearrange("p (h o) -> p h o", o=1)
                        .broadcast_to([128, 32, PD]), ALU.mult)
                    ctx.s_v.inc(v.tensor_tensor(S_out[:, sl], S_out[:, sl], tsc[:, sl], ALU.add))
                    ev[("supd", c, half)] = ctx.s_v.n

            for lt in range(2):
                @b.sync
                def _(s, c=c, lt=lt):
                    if (c, lt) != (0, 0):
                        pc, pl = (c, 0) if lt == 1 else (c - 1, 1)
                        s.wait_ge(ctx.s_v.h, ev[("gate", pc, pl)])
                    ctx.ld_sy.dma(s.dma_start(
                        szb[:, :], io.sz_tok[(2 * c + lt) * 128:(2 * c + lt + 1) * 128, :]))
                ev[("szld", c, lt)] = ctx.ld_sy.n

                for hg in range(8):
                    yi = yidx[0]

                    @b.tensor
                    def _(t, c=c, lt=lt, pp=pp, hg=hg, yi=yi):
                        if hg == 0:
                            t.wait_ge(ctx.s_v.h, ev[("ssc", c)])
                        if yi >= 2:
                            t.wait_ge(ctx.s_v.h, ev[("comb", yi - 2)])
                        bank = yi % 2
                        for st in range(2):
                            t.matmul(psy[:, bank, :], cbt[:, pp, st, lt * 128:(lt + 1) * 128],
                                     xb[:, st, hg * 512:(hg + 1) * 512],
                                     start=(st == 0), stop=False)
                        mm = t.matmul(psy[:, bank, :],
                                      sb.ct[:, c * 256 + lt * 128:c * 256 + (lt + 1) * 128],
                                      ssc[:, hg * 512:(hg + 1) * 512], start=False, stop=True)
                        ctx.s_t.inc(mm)
                        ev[("ymm", yi)] = ctx.s_t.n

                    @b.vector
                    def _(v, c=c, lt=lt, hg=hg, yi=yi):
                        if lt == 1 and hg == 0:
                            v.wait_ge(ctx.s_s.h, ev[("yn", c, 0)])
                        v.wait_ge(ctx.s_t.h, ev[("ymm", yi)])
                        ctx.s_v.inc(v.tensor_tensor(
                            yb.rearrange("p (h q) -> p h q", q=PD)[:, hg * 8:(hg + 1) * 8, :],
                            psy[:, yi % 2, :].rearrange("p (h q) -> p h q", q=PD)[:, :, :],
                            sb.expA[:, (2 * c + lt) * H + hg * 8:(2 * c + lt) * H + (hg + 1) * 8]
                            .rearrange("p (h o) -> p h o", o=1).broadcast_to([128, 8, PD]),
                            ALU.mult))
                        ev[("comb", yi)] = ctx.s_v.n
                    yidx[0] += 1

                @b.vector
                def _(v, c=c, lt=lt):
                    if lt == 1:
                        v.wait_ge(ctx.s_s.h, ev[("yn", c, 0)])
                    v.tensor_tensor(
                        tsc.rearrange("p (h q) -> p h q", q=PD)[:, :, :],
                        x2.rearrange("p s (h q) -> p s h q", q=PD)[:, lt, :, :],
                        sb.d_t.rearrange("p (h o) -> p h o", o=1).broadcast_to([128, H, PD]),
                        ALU.mult)
                    ctx.s_v.inc(v.tensor_tensor(yb[:, :], yb[:, :], tsc[:, :], ALU.add))
                    ev[("dterm", c, lt)] = ctx.s_v.n
                    v.wait_ge(ctx.ld_sy.h, ev[("szld", c, lt)])
                    ctx.s_v.inc(v.tensor_tensor(tsc[:, :], yb[:, :], szb[:, :], ALU.mult))
                    ev[("gate", c, lt)] = ctx.s_v.n

                @b.scalar
                def _(sc, c=c, lt=lt):
                    sc.wait_ge(ctx.s_v.h, ev[("gate", c, lt)])
                    ctx.s_s.inc(sc.activation(yb[:, :], tsc[:, :], AF.Square,
                                              accum_out=cols[:, 0 + lt:1 + lt]))
                    sc.wait_ge(ctx.s_s.h, ctx.s_s.n)
                    ctx.s_s.inc(sc.activation(cols[:, 2 + lt:3 + lt], cols[:, 0 + lt:1 + lt],
                                              AF.Identity, scale=1.0 / INTER,
                                              bias=cols[:, 8:9]))
                    sc.wait_ge(ctx.s_s.h, ctx.s_s.n)
                    ctx.s_s.inc(sc.activation(cols[:, 4 + lt:5 + lt], cols[:, 2 + lt:3 + lt],
                                              AF.Ln))
                    sc.wait_ge(ctx.s_s.h, ctx.s_s.n)
                    ctx.s_s.inc(sc.activation(cols[:, 6 + lt:7 + lt], cols[:, 4 + lt:5 + lt],
                                              AF.Exp, scale=-0.5))
                    sc.wait_ge(ctx.s_s.h, ctx.s_s.n)
                    ctx.s_s.inc(sc.activation(yb[:, :], tsc[:, :], AF.Identity,
                                              scale=cols[:, 6 + lt:7 + lt]))
                ev[("yn", c, lt)] = ctx.s_s.n

                @b.vector
                def _(v, c=c, lt=lt):
                    v.wait_ge(ctx.s_s.h, ev[("yn", c, lt)])
                    if (c, lt) != (0, 0):
                        pc, pl = (c, 0) if lt == 1 else (c - 1, 1)
                        v.wait_ge(ctx.st_sy.h, ev[("yfst", pc, pl)])
                    ctx.s_v.inc(v.tensor_tensor(yfb[:, :], yb[:, :], sb.nw[:, :], ALU.mult))
                ev[("yfb", c, lt)] = ctx.s_v.n

                @b.sync
                def _(s, c=c, lt=lt):
                    s.wait_ge(ctx.s_v.h, ev[("yfb", c, lt)])
                    ctx.st_sy.dma(s.dma_start(
                        io.yf_tok[(2 * c + lt) * 128:(2 * c + lt + 1) * 128, :], yfb[:, :]))
                ev[("yfst", c, lt)] = ctx.st_sy.n
                ctx.emit_p1(5)


def p7_wout(ctx, io, sb):
    nc, b = ctx.nc, ctx.block
    with (
        nc.sbuf_tensor("p7_w", [128, 32, 1024], BF16) as wsb,
        nc.sbuf_tensor("p7_yf", [128, 2, 32, 512], BF16) as yfT,
        nc.sbuf_tensor("p7_st", [128, 2, 1024], F32) as stg,
        nc.psum_tensor("p7_ps", [128, 4, 512], F32) as ps,
    ):
        ev = {}
        eidx = [0]

        scl = ctx.q_scl

        @b.sync
        def _(s):
            ctx.ld_sy.dma(s.dma_start(ctx.q_idb[:, 0:128], io.c_ident[:, :]))
        ev["idld"] = ctx.ld_sy.n

        @b.scalar
        def _(sc):
            sc.wait_ge(ctx.ld_sy.h, ev["idld"])
            ctx.s_s.inc(sc.copy(ctx.q_id32[:, 0:128], ctx.q_idb[:, 0:128]))
        ev["id32"] = ctx.s_s.n

        @b.vector
        def _(v):
            v.wait_ge(ctx.s_g.h, ctx.ev_p1done)
            mm = None
            for j in range(3):
                mm = v.tensor_reduce(scl[:, j:j + 1], ctx.q_acc[:, j * 128:(j + 1) * 128],
                                     AXX, ALU.add)
            ctx.s_v.inc(mm)
        ev["accred"] = ctx.s_v.n

        @b.tensor
        def _(t):
            t.wait_ge(ctx.s_v.h, ev["accred"])
            t.wait_ge(ctx.s_s.h, ev["id32"])
            psf = ps.rearrange("p a n -> p (a n)")
            ctx.s_t.inc(t.transpose(psf[0:4, 0:128], scl[:, 0:4], ctx.q_id32[:, 0:128]))
        ev["acstr"] = ctx.s_t.n
        ctx.ev_id32done = ev["id32"]
        ctx.ev_acstr = ev["acstr"]

        @b.vector
        def _(v):
            v.wait_ge(ctx.s_t.h, ev["acstr"])
            psf = ps.rearrange("p a n -> p (a n)")
            ctx.s_v.inc(v.tensor_reduce(scl[0:4, 4:5], psf[0:4, 0:128], AXX, ALU.add))
            v.wait_ge(ctx.s_v.h, ctx.s_v.n)
            ctx.s_v.inc(v.tensor_scalar(scl[0:4, 5:6], scl[0:4, 4:5], 1.0 / (DM * DE), 1e-8,
                                        ALU.mult, ALU.add))
            v.wait_ge(ctx.s_v.h, ctx.s_v.n)
            v.tensor_scalar(scl[0:4, 6:7], scl[0:4, 5:6], 0.5, None, ALU.mult)
            ctx.s_v.inc(v.tensor_scalar(scl[0:4, 7:8], scl[0:4, 5:6], -1.0, None, ALU.mult))
        ev["sclv"] = ctx.s_v.n

        @b.sync
        def _(s):
            s.wait_ge(ctx.s_v.h, ev["sclv"])
            ctx.st_sy.dma(s.dma_start(io.srow[2:6, 0:3], scl[0:4, 5:8]))
            s.wait_ge(ctx.st_sy.h, ctx.st_sy.n)
            for j in range(3):
                ctx.ld_sy.dma(s.dma_start(
                    scl[:, 3 * j + 7:3 * j + 10],
                    io.srow[2 + j:3 + j, 0:3].partition_broadcast(128)))
        ev["finread"] = ctx.s_v.n

        for half in range(2):
            @b.gpsimd
            def _(g, half=half):
                if half == 1:
                    g.wait_ge(ctx.s_t.h, ev[("mmall", 0)])
                for q in range(4):
                    ctx.ld_gp.dma(g.dma_start(
                        wsb[:, q * 8:(q + 1) * 8, :],
                        io.W_out.rearrange("(a p) d -> p a d", p=128)
                        [:, q * 8:(q + 1) * 8, half * 1024:(half + 1) * 1024]))
            ev[("wld", half)] = ctx.ld_gp.n

            for sl in range(8):
                @b.scalar
                def _(sc, half=half, sl=sl):
                    if half * 8 + sl >= 2:
                        sc.wait_ge(ctx.s_t.h, ev[("slmm", half * 8 + sl - 2)])
                    for k in range(32):
                        ctx.ld_sc.dma(sc.dma_start_transpose(
                            yfT[:, sl % 2, k, :],
                            io.yf_tok[sl * 512:(sl + 1) * 512, k * 128:(k + 1) * 128]))
                ev[("yfld", half, sl)] = ctx.ld_sc.n

                for tt in range(4):
                    ttg = (half * 8 + sl) * 4 + tt
                    for n in range(2):
                        e = eidx[0]

                        @b.tensor
                        def _(t, half=half, sl=sl, tt=tt, n=n, e=e):
                            if e % 8 == 0:
                                t.wait_ge(ctx.ld_sc.h, ev[("yfld", half, sl)])
                                t.wait_ge(ctx.ld_gp.h, ev[("wld", half)])
                            if e == 0:
                                t.wait_ge(ctx.s_v.h, ev["sclv"])
                            if e >= 4:
                                t.wait_ge(ctx.s_v.h, ev[("ev", e - 4)])
                            mm = None
                            for k in range(32):
                                mm = t.matmul(ps[:, e % 4, :],
                                              yfT[:, sl % 2, k, tt * 128:(tt + 1) * 128],
                                              wsb[:, k, n * 512:(n + 1) * 512],
                                              start=(k == 0), stop=(k == 31))
                            ctx.s_t.inc(mm)
                            ev[("mm", e)] = ctx.s_t.n

                        @b.vector
                        def _(v, ttg=ttg, n=n, e=e):
                            v.wait_ge(ctx.s_t.h, ev[("mm", e)])
                            if ttg >= 2:
                                v.wait_ge(ctx.st_gp.h, ev[("sst", ttg - 2)])
                            ctx.s_v.inc(v.tensor_copy(stg[:, ttg % 2, n * 512:(n + 1) * 512],
                                                      ps[:, e % 4, :]))
                            ev[("ev", e)] = ctx.s_v.n
                        eidx[0] += 1

                    @b.gpsimd
                    def _(g, half=half, sl=sl, tt=tt, ttg=ttg):
                        e = eidx[0] - 1
                        g.wait_ge(ctx.s_v.h, ev[("ev", e)])
                        tti = sl * 4 + tt
                        ctx.st_gp.dma(g.dma_start(
                            io.s_tok[tti * 128:(tti + 1) * 128, half * 1024:(half + 1) * 1024],
                            stg[:, ttg % 2, :]))
                        ev[("sst", ttg)] = ctx.st_gp.n
                    ctx.emit_p2(2)
                ev[("slmm", half * 8 + sl)] = ctx.s_t.n
            ev[("mmall", half)] = ctx.s_t.n


def p9_upgate(ctx, io):
    nc, b = ctx.nc, ctx.block
    with (
        nc.sbuf_tensor("p9_xnT", [128, 16, L], BF16) as xnT,
        nc.sbuf_tensor("p9_wq", [128, 2, 2, 8, 512], BF16) as wq,
        nc.sbuf_tensor("p9_hu", [128, 2, 512], F32) as hu,
        nc.sbuf_tensor("p9_hb", [128, 2, 512], BF16) as hb,
        nc.psum_tensor("p9_ps", [128, 4, 512], F32) as ps,
    ):
        ev = {}

        @b.scalar
        def _(sc):
            for sl in range(8):
                for k in range(16):
                    ctx.ld_sc.dma(sc.dma_start_transpose(
                        xnT[:, k, sl * 512:(sl + 1) * 512],
                        io.xn_tok[sl * 512:(sl + 1) * 512, k * 128:(k + 1) * 128]))
        ev["xnTsc"] = ctx.ld_sc.n

        qsrc = [io.q_upT, io.q_gateT]
        ei = [0]
        for eb in range(16):
            for sub in range(2):
                @b.sync
                def _(s, eb=eb, sub=sub):
                    if eb >= 1:
                        s.wait_ge(ctx.s_t.h, ev[("ebmm", eb - 1)])
                    s.wait_ge(ctx.st_gp.h, ctx.p2ev[("qcc", eb // 2)])
                    for mat in range(2):
                        for k in range(8):
                            ctx.ld_sy.dma(s.dma_start(
                                wq[:, sub, mat, k, :],
                                qsrc[mat][(sub * 8 + k) * 128:(sub * 8 + k + 1) * 128,
                                          eb * 512:(eb + 1) * 512]))
                ev[("qld", eb, sub)] = ctx.ld_sy.n

            waited1 = [False]
            for m in range(4):
                for n in range(8):
                    e = ei[0]

                    @b.tensor
                    def _(t, eb=eb, m=m, n=n, e=e):
                        if e % 32 == 0:
                            t.wait_ge(ctx.ld_sy.h, ev[("qld", eb, 0)])
                        if eb == 0 and e == 0:
                            t.wait_ge(ctx.ld_sc.h, ev["xnTsc"])
                        pp2 = e % 2
                        if e >= 2:
                            t.wait_ge(ctx.s_v.h, ev[("hev", e - 2)])
                        mm = None
                        for mat in range(2):
                            for k in range(16):
                                if k == 8 and not waited1[0]:
                                    t.wait_ge(ctx.ld_sy.h, ev[("qld", eb, 1)])
                                    waited1[0] = True
                                mm = t.matmul(ps[:, 2 * pp2 + mat, :],
                                              wq[:, k // 8, mat, k % 8,
                                                 m * 128:(m + 1) * 128],
                                              xnT[:, k, n * 512:(n + 1) * 512],
                                              start=(k == 0), stop=(k == 15))
                        ctx.s_t.inc(mm)
                        ev[("mmup", e)] = ctx.s_t.n

                    @b.scalar
                    def _(sc, e=e):
                        sc.wait_ge(ctx.s_t.h, ev[("mmup", e)])
                        if e >= 2:
                            sc.wait_ge(ctx.s_v.h, ev[("hev", e - 2)])
                        ctx.s_s.inc(sc.activation(hu[:, e % 2, :], ps[:, 2 * (e % 2), :],
                                                  AF.Silu))
                        ev[("silu", e)] = ctx.s_s.n

                    @b.vector
                    def _(v, e=e):
                        v.wait_ge(ctx.s_s.h, ev[("silu", e)])
                        if e >= 2:
                            v.wait_ge(ctx.st_gp.h, ev[("hst", e - 2)])
                        ctx.s_v.inc(v.tensor_tensor(hb[:, e % 2, :], hu[:, e % 2, :],
                                                    ps[:, 2 * (e % 2) + 1, :], ALU.mult))
                        ev[("hev", e)] = ctx.s_v.n

                    @b.gpsimd
                    def _(g, eb=eb, m=m, n=n, e=e):
                        g.wait_ge(ctx.s_v.h, ev[("hev", e)])
                        ctx.st_gp.dma(g.dma_start(
                            io.hid_cm[eb * 512 + m * 128:eb * 512 + (m + 1) * 128,
                                      n * 512:(n + 1) * 512],
                            hb[:, e % 2, :]))
                        ev[("hst", e)] = ctx.st_gp.n
                    ei[0] += 1
                    if (m * 8 + n) % 2 == 1:
                        ctx.emit_p2(1)
            ev[("ebmm", eb)] = ctx.s_t.n
        ctx.emit_p2(400)


def p10_down(ctx, io):
    nc, b = ctx.nc, ctx.block
    with (
        nc.sbuf_tensor("p10_qd", [128, 64, 2, 512], BF16) as qd,
        nc.sbuf_tensor("p10_h", [128, 2, 32, 256], BF16) as hd,
        nc.sbuf_tensor("p10_g", [128, 4, 512], F32) as gb,
        nc.sbuf_tensor("p10_s", [128, 4, 512], F32) as sxs,
        nc.sbuf_tensor("p10_x", [128, 4, 512], F32) as sxx,
        nc.sbuf_tensor("p10_o", [128, 4, 512], F32) as ob,
        nc.psum_tensor("p10_ps", [128, 8, 512], F32) as ps,
    ):
        ev = {}
        for half in range(2):
            @b.sync
            def _(s, half=half):
                if half == 0:
                    s.wait_ge(ctx.st_gp.h, ctx.ev_qdown)
                if half == 1:
                    s.wait_ge(ctx.s_t.h, ev[("tpmm", 15)])
                for ek in range(64):
                    for n in range(2):
                        ctx.ld_sy.dma(s.dma_start(
                            qd[:, ek, n, :],
                            io.q_downT[ek * 128:(ek + 1) * 128,
                                       half * 1024 + n * 512:half * 1024 + (n + 1) * 512]))
            ev[("qdld", half)] = ctx.ld_sy.n

            hid_r = io.hid_cm.rearrange("(a p) t -> p a t", p=128)
            for tp in range(16):
                idx = half * 16 + tp
                pp = idx % 2

                @b.gpsimd
                def _(g, half=half, tp=tp, idx=idx):
                    if idx >= 1:
                        g.wait_ge(ctx.s_t.h, ev[("tpmm_h", idx - 1)])
                    ctx.ld_gp.dma(g.dma_start(
                        hd[:, 0, :, :], hid_r[:, 0:32, tp * 256:(tp + 1) * 256]))
                    ev[("hld", idx, 0)] = ctx.ld_gp.n
                    if idx >= 1:
                        g.wait_ge(ctx.s_t.h, ev[("tpmm", idx - 1)])
                    ctx.ld_gp.dma(g.dma_start(
                        hd[:, 1, :, :], hid_r[:, 32:64, tp * 256:(tp + 1) * 256]))
                    ev[("hld", idx, 1)] = ctx.ld_gp.n

                @b.sync
                def _(s, half=half, tp=tp, idx=idx):
                    if idx >= 1:
                        s.wait_ge(ctx.s_v.h, ev[("oev", idx - 1)])
                    for ttj in range(2):
                        tti = tp * 2 + ttj
                        for n in range(2):
                            c0 = half * 1024 + n * 512
                            ctx.ld_sy.dma(s.dma_start(
                                sxs[:, 2 * ttj + n, :],
                                io.s_tok[tti * 128:(tti + 1) * 128, c0:c0 + 512]))
                            ctx.ld_sy.dma(s.dma_start(
                                sxx[:, 2 * ttj + n, :],
                                io.x[tti * 128:(tti + 1) * 128, c0:c0 + 512]))
                ev[("sld", idx)] = ctx.ld_sy.n

                @b.tensor
                def _(t, half=half, tp=tp, idx=idx, pp=pp):
                    t.wait_ge(ctx.ld_gp.h, ev[("hld", idx, 0)])
                    t.wait_ge(ctx.ld_sy.h, ev[("qdld", half)])
                    if idx >= 2:
                        t.wait_ge(ctx.s_v.h, ev[("oev", idx - 2)])
                    mm = None
                    for ek in range(32):
                        for ttj in range(2):
                            for n in range(2):
                                mm = t.matmul(ps[:, 4 * pp + 2 * ttj + n, :],
                                              hd[:, 0, ek, ttj * 128:(ttj + 1) * 128],
                                              qd[:, ek, n, :],
                                              start=(ek == 0), stop=False)
                    ctx.s_t.inc(mm)
                    ev[("tpmm_h", idx)] = ctx.s_t.n
                    t.wait_ge(ctx.ld_gp.h, ev[("hld", idx, 1)])
                    for ek in range(32, 64):
                        for ttj in range(2):
                            for n in range(2):
                                mm = t.matmul(ps[:, 4 * pp + 2 * ttj + n, :],
                                              hd[:, 1, ek - 32, ttj * 128:(ttj + 1) * 128],
                                              qd[:, ek, n, :],
                                              start=False, stop=(ek == 63))
                    ctx.s_t.inc(mm)
                ev[("tpmm", idx)] = ctx.s_t.n

                @b.scalar
                def _(sc, half=half, tp=tp, idx=idx, pp=pp):
                    sc.wait_ge(ctx.s_t.h, ev[("tpmm", idx)])
                    for u in range(4):
                        if idx >= 1:
                            sc.wait_ge(ctx.s_v.h, ev[("guse", idx - 1, u)])
                        ctx.s_s.inc(sc.activation(gb[:, u, :], ps[:, 4 * pp + u, :], AF.Silu))
                        ev[("g", idx, u)] = ctx.s_s.n

                @b.vector
                def _(v, half=half, tp=tp, idx=idx):
                    v.wait_ge(ctx.ld_sy.h, ev[("sld", idx)])
                    for u in range(4):
                        v.wait_ge(ctx.s_s.h, ev[("g", idx, u)])
                        if idx >= 1:
                            v.wait_ge(ctx.st_gp.h, ev[("ost", idx - 1, u)])
                        v.tensor_tensor(ob[:, u, :], gb[:, u, :], sxs[:, u, :], ALU.mult)
                        ctx.s_v.inc(v.tensor_tensor(ob[:, u, :], ob[:, u, :],
                                                    sxx[:, u, :], ALU.add))
                        ev[("guse", idx, u)] = ctx.s_v.n
                        ev[("oevu", idx, u)] = ctx.s_v.n
                    ev[("oev", idx)] = ctx.s_v.n

                @b.gpsimd
                def _(g, half=half, tp=tp, idx=idx):
                    for u in range(4):
                        ttj, n = u // 2, u % 2
                        tti = tp * 2 + ttj
                        c0 = half * 1024 + n * 512
                        g.wait_ge(ctx.s_v.h, ev[("oevu", idx, u)])
                        ctx.st_gp.dma(g.dma_start(
                            io.out[tti * 128:(tti + 1) * 128, c0:c0 + 512], ob[:, u, :]))
                        ev[("ost", idx, u)] = ctx.st_gp.n
